# version 10
# baseline (speedup 1.0000x reference)
import os
import numpy as np
import ml_dtypes
from contextlib import ExitStack

import concourse.bass as bass
import concourse.mybir as mybir
from concourse.bass_utils import run_bass_kernel_spmd

F32 = mybir.dt.float32
BF16 = mybir.dt.bfloat16
I32 = mybir.dt.int32
AF = mybir.ActivationFunctionType
ALU = mybir.AluOpType
AX = mybir.AxisListType
BF = ml_dtypes.bfloat16

NCORES = 8
S = 2048
D = 2048
HD = 128
MLP0 = 16384
MLP1 = 8192
RECENT = 204
HEAVY = 204
NEG = -1.0e9
EPS = 1e-5



def _rope_tables(theta, scale):
    inv = 1.0 / theta ** (np.arange(0, HD, 2, dtype=np.float32) / HD)
    pos = np.arange(S, dtype=np.float32)
    fr = pos[:, None] * inv[None, :]
    emb = np.concatenate([fr, fr], -1)
    cos = (np.cos(emb) * scale).astype(np.float32)
    sgn = np.ones(HD, np.float32)
    sgn[: HD // 2] = -1.0
    s2 = (np.sin(emb) * scale * sgn[None, :]).astype(np.float32)
    return cos.T.astype(BF).copy(), s2.T.astype(BF).copy()


def host_consts():
    c = {}
    c["ident"] = np.eye(128, dtype=BF)
    c["identf"] = np.eye(128, dtype=np.float32)
    c["ones_1x128b"] = np.ones((1, 128), dtype=BF)
    c["ones_1x128f"] = np.ones((1, 128), np.float32)
    c["ones_128x1b"] = np.ones((128, 1), dtype=BF)
    R = np.zeros((64, D), np.float32)
    for b in range(64):
        R[b, b * 32:(b + 1) * 32] = 1.0
    c["R64"] = R
    c["R64h"] = R * 0.5
    sc = 1.0 / np.sqrt(HD)
    c["cq0"], c["sq0"] = _rope_tables(10000.0, sc)
    c["ck0"], c["sk0"] = _rope_tables(10000.0, 1.0)
    c["cq1"], c["sq1"] = _rope_tables(1000000.0, sc)
    c["ck1"], c["sk1"] = _rope_tables(1000000.0, 1.0)
    p = np.arange(128)
    cc = np.arange(512)
    for d0 in range(-3, 4):
        dl = d0 + cc // 128
        tmj = 128 * dl[None, :] + (cc % 128)[None, :] - p[:, None]
        causal = tmj >= 0
        band = causal & (tmj <= RECENT)
        c[f"pat{d0}"] = np.where(band, 0.0, NEG).astype(np.float32)
        c[f"dg{d0}"] = np.broadcast_to(np.where(dl >= 1, 0.0, NEG)[None, :], (128, 512)).astype(np.float32).copy()
        c[f"cp{d0}"] = np.where(causal, 0.0, NEG).astype(np.float32)
    pA = np.arange(128)
    c["cpA"] = np.where(pA[None, :] <= pA[:, None], 0.0, NEG).astype(np.float32)[: , :]
    return c


CONST_SPECS = {
    "ident": ([128, 128], BF16),
    "identf": ([128, 128], F32),
    "ones_1x128b": ([1, 128], BF16),
    "ones_1x128f": ([1, 128], F32),
    "ones_128x1b": ([128, 1], BF16),
    "R64": ([64, D], F32),
    "R64h": ([64, D], F32),
    "cq0": ([128, S], BF16), "sq0": ([128, S], BF16),
    "ck0": ([128, S], BF16), "sk0": ([128, S], BF16),
    "cq1": ([128, S], BF16), "sq1": ([128, S], BF16),
    "ck1": ([128, S], BF16), "sk1": ([128, S], BF16),
}
CONST_SPECS["cpA"] = ([128, 128], F32)
for _d0 in range(-3, 4):
    CONST_SPECS[f"pat{_d0}"] = ([128, 512], F32)
    CONST_SPECS[f"dg{_d0}"] = ([128, 512], F32)
    CONST_SPECS[f"cp{_d0}"] = ([128, 512], F32)

_HC = None


def get_host_consts():
    global _HC
    if _HC is None:
        _HC = host_consts()
    return _HC


ENGINES = ("sync", "gpsimd", "vector", "scalar", "tensor")


class Sem:
    __slots__ = ("h", "n", "name")

    def __init__(self, h, name):
        self.h = h
        self.n = 0
        self.name = name

    def at(self, v):
        return (self, v)


class B:
    def __init__(self, nc, stack):
        self.nc = nc
        self.stack = stack
        self.steps = []
        self._semid = 0
        self._scratch = {}

    def sem(self, name=None):
        self._semid += 1
        nm = name or f"s{self._semid}"
        if getattr(self, "share_sems", False):
            key = nm.split("_", 1)[-1]
            reg = self._semreg = getattr(self, "_semreg", {})
            if key in reg:
                return reg[key]
            h = self.stack.enter_context(self.nc.semaphore(key))
            s = Sem(h, key)
            reg[key] = s
            return s
        h = self.stack.enter_context(self.nc.semaphore(nm))
        return Sem(h, nm)

    def sbuf(self, name, shape, dtype, scope=None):
        return (scope or self.stack).enter_context(self.nc.sbuf_tensor(name, shape, dtype))

    def psum(self, name, shape, dtype=F32, scope=None):
        return (scope or self.stack).enter_context(self.nc.psum_tensor(name, shape, dtype))

    def dram(self, name, shape, dtype, kind):
        return self.nc.dram_tensor(name, shape, dtype, kind=kind).ap()

    def set_entry_guard(self, guard):
        self._eg = {e: list(guard) for e in ENGINES}

    def on(self, eng, fn, wait=(), inc=None, k=1, is_dma=False):
        ws = []
        for w in wait:
            if isinstance(w, Sem):
                ws.append((w, w.n))
            else:
                ws.append((w[0], w[1]))
        eg = getattr(self, "_eg", None)
        if eg and eng in eg:
            for (s_, v_) in eg.pop(eng):
                ws.append((s_, v_))
        pair = None
        if inc is not None:
            inc.n += k
            pair = (inc, k, inc.n)
        self.steps.append((eng, ws, fn, pair, is_dma))

    def dma(self, eng, out, in_, wait=(), inc=None, transpose=False):
        self.on(eng, (lambda e, o=out, i=in_, t=transpose:
                      e.dma_start(out=o, in_=i, transpose=t)), wait=wait, inc=inc, k=16,
                is_dma=True)

    def autochain(self):
        autos = {e: self.sem(f"auto_{e}") for e in ("vector", "scalar", "gpsimd")}
        last = {}
        out = []
        for (eng, ws, fn, pair, is_dma) in self.steps:
            if eng in autos and not is_dma:
                if eng in last:
                    ws = list(ws) + [last[eng]]
                if pair is None:
                    a = autos[eng]
                    a.n += 1
                    pair = (a, 1, a.n)
                last[eng] = (pair[0], pair[2])
            out.append((eng, ws, fn, pair, is_dma))
        self.steps = out

    def check_deadlock(self, final_waits=()):
        qs = {e: [] for e in ENGINES}
        for st in self.steps:
            qs[st[0]].append(st)
        heads = {e: 0 for e in ENGINES}
        vals = {}
        progress = True
        while progress:
            progress = False
            for e in ENGINES:
                while heads[e] < len(qs[e]):
                    (_, ws, fn, pair, _d) = qs[e][heads[e]]
                    if all(vals.get(id(sem), 0) >= v for (sem, v) in ws):
                        if pair is not None:
                            vals[id(pair[0])] = vals.get(id(pair[0]), 0) + pair[1]
                        heads[e] += 1
                        progress = True
                    else:
                        break
        stuck = {e: len(qs[e]) - heads[e] for e in ENGINES if heads[e] < len(qs[e])}
        if stuck:
            for e in ENGINES:
                if heads[e] < len(qs[e]):
                    (_, ws, fn, pair, _d) = qs[e][heads[e]]
                    bad = [(sem.name, v, vals.get(id(sem), 0)) for (sem, v) in ws
                           if vals.get(id(sem), 0) < v]
                    print(f"DEADLOCK {e} step#{heads[e]}/{len(qs[e])} waits {bad}")
            raise RuntimeError(f"deadlock: {stuck}")
        for w in final_waits:
            sem, v = (w, w.n) if isinstance(w, Sem) else (w[0], w[1])
            if vals.get(id(sem), 0) < v:
                raise RuntimeError(f"final wait unreachable: {sem.name} {v} > {vals.get(id(sem), 0)}")

    def finalize(self, final_waits=()):
        self.autochain()
        self.check_deadlock(final_waits)
        block = self.stack.enter_context(self.nc.Block())
        per = {e: [] for e in ENGINES}
        for st in self.steps:
            per[st[0]].append(st)
        fw = []
        for w in final_waits:
            if isinstance(w, Sem):
                fw.append((w, w.n))
            else:
                fw.append((w[0], w[1]))

        def emit(engine_obj, name):
            last = {}
            for (_, ws, fn, pair, _d) in per[name]:
                for (sem, v) in ws:
                    if v > 0 and last.get(id(sem), -1) < v:
                        engine_obj.wait_ge(sem.h, v)
                        last[id(sem)] = v
                inst = fn(engine_obj)
                if pair is not None:
                    inst.then_inc(pair[0].h, pair[1])
            if name == "sync":
                for (sem, v) in fw:
                    engine_obj.wait_ge(sem.h, v)

        @block.sync
        def _(e):
            emit(e, "sync")

        @block.gpsimd
        def _(e):
            emit(e, "gpsimd")

        @block.vector
        def _(e):
            emit(e, "vector")

        @block.scalar
        def _(e):
            emit(e, "scalar")

        @block.tensor
        def _(e):
            emit(e, "tensor")


def load_consts(b, names, scope=None):
    if not hasattr(b, "_cdram"):
        b._cdram, b._cseq = {}, 0
    if not hasattr(b, "_cldsem"):
        b._cldsem = b.sem("cld")
    ld = b._cldsem
    aps = {}
    for n in names:
        shape, dt = CONST_SPECS[n]
        if n not in b._cdram:
            b._cdram[n] = b.dram("c_" + n, shape, dt, "ExternalInput")
        b._cseq += 1
        til = b.sbuf(f"sc{b._cseq}_{n}", shape, dt, scope)
        b.dma("sync", til[:], b._cdram[n], inc=ld)
        aps[n] = til
    return aps, ld


def const_inputs(names):
    hc = get_host_consts()
    return {"c_" + n: hc[n] for n in names}


def make_fc(b, values):
    til = b.sbuf("fconsts", [128, max(1, len(values))], F32)
    sem = b.sem("fc")
    aps = {}
    for i, v in enumerate(values):
        col = til[:, i:i + 1]
        b.on("vector", (lambda e, cl=col, vv=v: e.memset(cl, vv)), inc=sem)
        aps[v] = col
    return aps, sem


def build_prep1():
    nc = bass.Bass("TRN2", target_bir_lowering=False)
    stack = ExitStack()
    with stack:
        b = B(nc, stack)
        w1s = b.dram("w1s", [MLP1 // 8, D], F32, "ExternalInput")
        w2s = b.dram("w2s", [D // 8, MLP1], F32, "ExternalInput")
        out = b.dram("sums", [1, 2], F32, "ExternalOutput")

        xt = [b.sbuf(f"xt{i}", [128, MLP1], F32) for i in range(2)]
        part = b.sbuf("part", [128, 16], F32)
        acc = b.sbuf("acc", [128, 2], F32)
        row = b.sbuf("row", [1, 2], F32)
        ones = b.sbuf("onesb", [128, 1], F32)
        pr = b.psum("pr", [1, 2], F32)
        ld = b.sem("ld")
        cm = b.sem("cm")
        st = b.sem("st")

        b.on("vector", lambda e: e.memset(ones[:], 1.0), inc=cm)
        ldp = [b.sem(f"ldp{i}") for i in range(2)]
        tiles = [(w1s, i, D, 0) for i in range(8)] + [(w2s, i, MLP1, 1) for i in range(2)]
        for j, (src, i, w, col) in enumerate(tiles):
            buf = xt[j % 2]
            b.dma("gpsimd", buf[:, :w], src[i * 128:(i + 1) * 128, :],
                  wait=[cm.at(1 + max(0, (j - 2) + 1))], inc=ldp[j % 2])
            b.on("vector", (lambda e, bu=buf, ww=w, jj=j:
                            e.tensor_reduce(part[:, jj:jj + 1], bu[:, :ww], AX.X, ALU.add,
                                            apply_absolute_value=True)),
                 wait=[ldp[j % 2].at(ldp[j % 2].n)], inc=cm)
        b.on("vector", lambda e: e.tensor_reduce(acc[:, 0:1], part[:, 0:8], AX.X, ALU.add),
             wait=[cm], inc=cm)
        b.on("vector", lambda e: e.tensor_reduce(acc[:, 1:2], part[:, 8:10], AX.X, ALU.add),
             inc=cm)
        b.on("tensor", lambda e: e.matmul(pr[:], ones[:], acc[:], start=True, stop=True),
             wait=[cm], inc=cm)
        b.on("vector", lambda e: e.tensor_copy(row[:], pr[:]), wait=[cm], inc=cm)
        b.dma("sync", out, row[:], wait=[cm], inc=st)
        b.finalize(final_waits=[st.at(16)])
    return nc


def build_prep2():
    nc = bass.Bass("TRN2", target_bir_lowering=False)
    stack = ExitStack()
    with stack:
        b = B(nc, stack)
        C, cld = load_consts(b, ["identf", "ident", "ones_1x128f", "R64", "R64h"])

        gWp = b.dram("gWp", [16 * 128, 1024], I32, "ExternalInput")
        gSs = b.dram("gSs", [16 * 128, 64], F32, "ExternalInput")
        uWp = b.dram("uWp", [16 * 128, 1024], I32, "ExternalInput")
        uSs = b.dram("uSs", [16 * 128, 64], F32, "ExternalInput")
        dWp = b.dram("dWp", [D, 1024], I32, "ExternalInput")
        dSs = b.dram("dSs", [D, 64], F32, "ExternalInput")
        absums = b.dram("absums", [1, 16], F32, "ExternalInput")
        w1p = b.dram("w1p", [8, 128, D], F32, "ExternalInput")
        w2p = b.dram("w2p", [8, 128, D], F32, "ExternalInput")
        q0p = b.dram("q0p", [2, 128, D], F32, "ExternalInput")
        k0p = b.dram("k0p", [2, 128, D], F32, "ExternalInput")
        v0p = b.dram("v0p", [1, 128, MLP1], F32, "ExternalInput")
        o0p = b.dram("o0p", [2, 128, D], F32, "ExternalInput")
        q1p = b.dram("q1p", [2, 128, D], F32, "ExternalInput")
        k1p = b.dram("k1p", [1, 128, D], F32, "ExternalInput")
        v1p = b.dram("v1p", [1, 128, MLP1], F32, "ExternalInput")
        o1p = b.dram("o1p", [2, 128, D], F32, "ExternalInput")

        gT = b.dram("gT", [16, 128, D], BF16, "ExternalOutput")
        uT = b.dram("uT", [16, 128, D], BF16, "ExternalOutput")
        dT = b.dram("dT", [16, 128, D], BF16, "ExternalOutput")
        w1T = b.dram("w1T", [8, 128, D], BF16, "ExternalOutput")
        w2T = b.dram("w2T", [8, 128, D], BF16, "ExternalOutput")
        q0T = b.dram("q0T", [2, 128, D], BF16, "ExternalOutput")
        k0T = b.dram("k0T", [2, 128, D], BF16, "ExternalOutput")
        v0T = b.dram("v0T", [1, 128, MLP1], BF16, "ExternalOutput")
        o0T = b.dram("o0T", [2, 128, D], BF16, "ExternalOutput")
        q1T = b.dram("q1T", [2, 128, D], BF16, "ExternalOutput")
        k1T = b.dram("k1T", [1, 128, D], BF16, "ExternalOutput")
        v1T = b.dram("v1T", [1, 128, MLP1], BF16, "ExternalOutput")
        o1T = b.dram("o1T", [2, 128, D], BF16, "ExternalOutput")

        srow = b.sbuf("srow", [1, 16], F32)
        tau = b.sbuf("tau", [128, 2], F32)
        taur = b.sbuf("taur", [1, 2], F32)
        taurb = b.sbuf("taurb", [1, 2], BF16)
        scope1 = ExitStack()
        pk = [b.sbuf(f"pk{i}", [128, 1024], I32, scope1) for i in range(2)]
        ssc = [b.sbuf(f"ssc{i}", [128, 64], F32, scope1) for i in range(2)]
        wi = b.sbuf("wi", [128, 1024], I32, scope1)
        c0 = b.sbuf("c0", [128, 1024], F32, scope1)
        c1 = b.sbuf("c1", [128, 1024], F32, scope1)
        t1 = b.sbuf("t1", [128, 1024], F32, scope1)
        t2 = b.sbuf("t2", [128, 1024], F32, scope1)
        deqv = b.sbuf("deqv", [128, D], F32, scope1)
        wtile = [b.sbuf(f"wtile{i}", [128, D], BF16, scope1) for i in range(2)]
        sct = b.sbuf("sct", [64, 128], F32, scope1)
        tbuf = [b.sbuf(f"tbuf{i}", [128, D], BF16, scope1) for i in range(2)]
        dpan = b.sbuf("dpan", [128, 16 * D], BF16, scope1)

        pt = b.psum("pt", [64, 128], F32)
        pr = [b.psum(f"pr{i}", [128, 512], F32) for i in range(4)]
        ptr = [b.psum(f"ptr{i}", [128, 128], BF16) for i in range(2)]

        LDp = [b.sem(f"LD{i}") for i in range(2)]
        PEA = b.sem("PEA")
        VDa = b.sem("VDa")
        VL = b.sem("VL")
        TE = b.sem("TE")
        ST = b.sem("ST")
        TAU = b.sem("TAU")

        ALD = b.sem("ALD")
        b.dma("sync", srow[:], absums, inc=ALD)
        b.on("vector", lambda e: e.tensor_reduce(taur[:, 0:1], srow[:, 0:16:2], AX.X, ALU.add),
             wait=[ALD.at(16)])
        b.on("vector", lambda e: e.tensor_reduce(taur[:, 1:2], srow[:, 1:16:2], AX.X, ALU.add))
        b.on("vector", lambda e: e.tensor_scalar(taur[:, 0:1], taur[:, 0:1],
                                                 0.7 / (MLP1 * D), None, ALU.mult))
        b.on("vector", lambda e: e.tensor_scalar(taur[:, 1:2], taur[:, 1:2],
                                                 0.7 / (MLP1 * D), None, ALU.mult))
        b.on("vector", lambda e: e.tensor_copy(taurb[:], taur[:]), inc=TAU)
        b.on("tensor", lambda e: e.matmul(pr[0][:, 0:2], C["ones_1x128f"][0:1, :], taur[:],
                                          start=True, stop=True),
             wait=[TAU.at(1), cld], inc=TAU)
        b.on("vector", lambda e: e.tensor_copy(tau[:], pr[0][0:128, 0:2]), wait=[TAU.at(2)], inc=TAU)

        njobs = 48
        store_done = {}
        te_done = {}
        vl_done = {}
        pe_done = {}

        def dequant_tile(j, pksrc, scsrc, scale_R, writer):
            pb, sb = pk[j % 2], ssc[j % 2]
            LDc = LDp[j % 2]
            w = [] if j < 2 else [VL.at(vl_done[j - 2]), PEA.at(pe_done[j - 2])]
            b.dma("gpsimd", pb[:], pksrc, wait=w, inc=LDc)
            b.dma("gpsimd", sb[:], scsrc, inc=LDc)
            ldv = LDc.n
            wt = [LDc.at(ldv), cld] + ([] if j < 1 else [VDa.at(j)])
            if j == 0:
                wt.append(TAU.at(3))
            b.on("tensor", (lambda e, s=sb: e.transpose(pt[:], s[:], C["identf"][:])),
                 wait=wt, inc=PEA)
            pv = PEA.n
            b.on("vector", (lambda e: e.tensor_copy(sct[:], pt[:])), wait=[PEA.at(pv)], inc=VDa)
            wr = [VDa.at(VDa.n)] + ([] if j < 1 else [VL.at(vl_done[j - 1])])
            if j == 0:
                wr.append(TAU.at(3))
            for q in range(4):
                b.on("tensor", (lambda e, qq=q, R=scale_R:
                                e.matmul(pr[qq][:], sct[:], R[:, qq * 512:(qq + 1) * 512],
                                         start=True, stop=True)),
                     wait=wr if q == 0 else (), inc=PEA)
            pe_done[j] = PEA.n
            n = 1024

            def v(fn, wait=(), inc=None):
                b.on("vector", fn, wait=wait, inc=inc)

            v(lambda e: e.tensor_scalar(wi[:, :n], pb[:], 15, None, ALU.bitwise_and),
              wait=[LDc.at(ldv)])
            v(lambda e: e.tensor_copy(c0[:, :n], wi[:, :n]))
            v(lambda e: e.tensor_scalar(wi[:, :n], pb[:], 4, None, ALU.logical_shift_right))
            v(lambda e: e.tensor_scalar(wi[:, :n], wi[:, :n], 15, None, ALU.bitwise_and))
            v(lambda e: e.tensor_copy(c1[:, :n], wi[:, :n]))
            for src, off in ((c0, 0), (c1, 1)):
                sv = deqv[:, off:2 * n:2]
                v(lambda e, s=src: e.tensor_scalar(t1[:, :n], s[:, :n], 7.5, 8.0, ALU.is_ge, ALU.mult))
                v(lambda e, s=src: e.tensor_tensor(t2[:, :n], s[:, :n], t1[:, :n], ALU.subtract))
                v(lambda e, o=sv: e.tensor_scalar(o, t2[:, :n], 0.5, -2.0, ALU.mult, ALU.add))
                v(lambda e, o=sv: e.tensor_scalar(o, o, 0.0, None, ALU.max))
                v(lambda e, o=sv: e.scalar_tensor_tensor(o, t2[:, :n], 0.5, o, ALU.mult, ALU.add))
                v(lambda e: e.tensor_scalar(t2[:, :n], t2[:, :n], -6.0, 0.0, ALU.add, ALU.max))
                v(lambda e, o=sv: e.tensor_tensor(o, o, t2[:, :n], ALU.add))
                v(lambda e: e.tensor_scalar(t1[:, :n], t1[:, :n], -0.25, 1.0, ALU.mult, ALU.add))
                v(lambda e, o=sv: e.tensor_tensor(o, o, t1[:, :n], ALU.mult))
            wm = [PEA.at(pe_done[j])] + ([] if j < 2 else [TE.at(te_done[j - 2])])
            wt_ = wtile[j % 2]
            for q in range(4):
                v((lambda e, qq=q, w_=wt_: e.tensor_tensor(w_[:, qq * 512:(qq + 1) * 512],
                                                           deqv[:, qq * 512:(qq + 1) * 512],
                                                           pr[qq][:], ALU.mult)),
                  wait=wm if q == 0 else (), inc=(VL if q == 3 else None))
            vl_done[j] = VL.n
            writer(j, wt_)

        STp = getattr(b, "_stp", None)
        if STp is None:
            STp = [b.sem(f"STp{i}") for i in range(2)]
            b._stp = STp

        def panel_writer(dst3d, idx):
            def w(j, wt_):
                tb = tbuf[j % 2]
                wstore = [] if ("pstore", j - 2) not in store_done else \
                    [STp[j % 2].at(store_done[("pstore", j - 2)])]
                for r in range(16):
                    pp = ptr[r % 2]
                    wpe = [VL.at(vl_done[j])]
                    if r >= 2:
                        wpe.append(TE.at(TE.n - 1))
                    if r == 0:
                        wpe += wstore
                        if TE.n > 0:
                            wpe.append(TE.at(TE.n))
                    b.on("tensor", (lambda e, p=pp, w2_=wt_, rr=r:
                                    e.transpose(p[:], w2_[:, rr * 128:(rr + 1) * 128], C["ident"][:])),
                         wait=wpe, inc=PEA)
                    b.on("scalar", (lambda e, p=pp, t=tb, rr=r:
                                    e.copy(t[:, rr * 128:(rr + 1) * 128], p[:])),
                         wait=[PEA.at(PEA.n)], inc=TE)
                te_done[j] = TE.n
                pe_done[j] = PEA.n
                b.dma("sync", dst3d[idx], tb[:], wait=[TE.at(TE.n)], inc=STp[j % 2])
                store_done[("pstore", j)] = STp[j % 2].n
            return w

        def down_writer(q):
            def w(j, wt_):
                for r in range(16):
                    pp = ptr[r % 2]
                    wpe = [VL.at(vl_done[j])]
                    if r >= 2:
                        wpe.append(TE.at(TE.n - 1))
                    if r == 0 and TE.n > 0:
                        wpe.append(TE.at(TE.n))
                    b.on("tensor", (lambda e, p=pp, w2_=wt_, rr=r:
                                    e.transpose(p[:], w2_[:, rr * 128:(rr + 1) * 128], C["ident"][:])),
                         wait=wpe, inc=PEA)
                    b.on("scalar", (lambda e, p=pp, rr=r, qq=q:
                                    e.copy(dpan[:, rr * D + qq * 128: rr * D + (qq + 1) * 128], p[:])),
                         wait=[PEA.at(PEA.n)], inc=TE)
                te_done[j] = TE.n
                pe_done[j] = PEA.n
            return w

        j = 0
        for i in range(16):
            dequant_tile(j, gWp[i * 128:(i + 1) * 128, :], gSs[i * 128:(i + 1) * 128, :],
                         C["R64"], panel_writer(gT, i))
            j += 1
        for i in range(16):
            dequant_tile(j, uWp[i * 128:(i + 1) * 128, :], uSs[i * 128:(i + 1) * 128, :],
                         C["R64"], panel_writer(uT, i))
            j += 1
        for q in range(16):
            dequant_tile(j, dWp[q * 128:(q + 1) * 128, :], dSs[q * 128:(q + 1) * 128, :],
                         C["R64h"], down_writer(q))
            j += 1
        te_all = TE.n
        DTS = b.sem("DTS")
        for i in range(16):
            b.dma("sync", dT[i], dpan[:, i * D:(i + 1) * D], wait=[TE.at(te_all)], inc=DTS)

        scope1.close()
        dq_barrier = [STp[0].at(STp[0].n), STp[1].at(STp[1].n), DTS.at(DTS.n),
                      VL.at(VL.n), PEA.at(PEA.n), TE.at(TE.n),
                      VDa.at(VDa.n), LDp[0].at(LDp[0].n), LDp[1].at(LDp[1].n)]
        fpan = [b.sbuf(f"fpan{i}", [128, MLP1], F32) for i in range(2)]
        opan = [b.sbuf(f"opan{i}", [128, MLP1], BF16) for i in range(2)]
        tern_ab = b.sbuf("tern_ab", [128, D], F32)
        tern_sg = b.sbuf("tern_sg", [128, D], F32)
        PLp = [b.sem(f"PL{i}") for i in range(2)]
        SPp = [b.sem(f"SP{i}") for i in range(2)]
        SC2 = b.sem("SC2")
        VT = b.sem("VT")
        jobs_done = {}
        jt = [0]

        def tern(srcp, dstp, n, width, taucol, half):
            for i in range(n):
                jj = jt[0]
                jt[0] += 1
                fp, op = fpan[jj % 2], opan[jj % 2]
                wld = (list(dq_barrier) if jj <= 1 else []) + (
                    [] if jj < 2 else [SC2.at(jobs_done[("sc", jj - 2)]),
                                       SPp[jj % 2].at(jobs_done[("st", jj - 2)])])
                b.dma("gpsimd", fp[:, :width], srcp[i], wait=wld, inc=PLp[jj % 2])
                pldv = PLp[jj % 2].n
                wab = [PLp[jj % 2].at(pldv)] + ([] if jj < 1 else [VT.at(jobs_done[("vt", jj - 1)])])
                b.on("scalar", (lambda e, f=fp, ww=width: e.activation(
                    tern_ab[:, :ww], f[:, :ww], AF.Abs)), wait=wab, inc=SC2)
                b.on("scalar", (lambda e, f=fp, ww=width: e.activation(
                    tern_sg[:, :ww], f[:, :ww], AF.Sign)), inc=SC2)
                jobs_done[("sc", jj)] = SC2.n
                m2 = 0.5 if half else 1.0
                b.on("vector", (lambda e, ww=width, tc=taucol, mm=m2: e.tensor_scalar(
                    tern_ab[:, :ww], tern_ab[:, :ww], tc, mm, ALU.is_ge, ALU.mult)),
                    wait=[SC2.at(SC2.n), TAU.at(3)], inc=VT)
                b.on("vector", (lambda e, o=op, ww=width: e.tensor_tensor(
                    o[:, :ww], tern_ab[:, :ww], tern_sg[:, :ww], ALU.mult)), inc=VT)
                jobs_done[("vt", jj)] = VT.n
                b.dma("sync", dstp[i], op[:, :width], wait=[VT.at(VT.n)], inc=SPp[jj % 2])
                jobs_done[("st", jj)] = SPp[jj % 2].n

        def cast(srcp, dstp, n, width=D):
            for i in range(n):
                jj = jt[0]
                jt[0] += 1
                fp, op = fpan[jj % 2], opan[jj % 2]
                wld = (list(dq_barrier) if jj <= 1 else []) + (
                    [] if jj < 2 else [VT.at(jobs_done[("vt", jj - 2)]),
                                       SPp[jj % 2].at(jobs_done[("st", jj - 2)])])
                b.dma("gpsimd", fp[:, :width], srcp[i], wait=wld, inc=PLp[jj % 2])
                b.on("vector", (lambda e, f=fp, o=op, w_=width: e.tensor_copy(o[:, :w_], f[:, :w_])),
                     wait=[PLp[jj % 2].at(PLp[jj % 2].n)], inc=VT)
                jobs_done[("vt", jj)] = VT.n
                jobs_done[("sc", jj)] = SC2.n
                b.dma("sync", dstp[i], op[:, :width], wait=[VT.at(VT.n)], inc=SPp[jj % 2])
                jobs_done[("st", jj)] = SPp[jj % 2].n

        tern(w1p, w1T, 8, D, tau[:, 0:1], False)
        tern(w2p, w2T, 8, D, tau[:, 1:2], True)
        cast(q0p, q0T, 2)
        cast(k0p, k0T, 2)
        cast(v0p, v0T, 1, MLP1)
        cast(o0p, o0T, 2)
        cast(q1p, q1T, 2)
        cast(k1p, k1T, 1)
        cast(v1p, v1T, 1, MLP1)
        cast(o1p, o1T, 2)

        b.finalize(final_waits=[ST])
    return nc


def emit_ffn(b, C, fc, ns, stream_in, stream_out, nmlp, gph, uph, dph,
             b1ap=None, b2ap=None, debug_stage="full", debug_tb=4):
    nt_all = nmlp // 128
    TB = 512
    NB = 16
    nbund = nt_all // NB
    is_l1 = uph is None

    sc = ExitStack()

    def sb(nm, shape, dt):
        return b.sbuf(ns + nm, shape, dt, sc)

    xt = [sb(f"xt{i}", [128, D], F32) for i in range(2)]
    sq = sb("sq", [128, D], F32)
    xhat = sb("xhat", [128, D], BF16)
    xhT = sb("xhT", [128, 16 * TB], BF16)
    wg = [sb(f"wg{i}", [128, D], BF16) for i in range(2)]
    wu = [sb(f"wu{i}", [128, D], BF16) for i in range(2)] if not is_l1 else None
    wd = [sb(f"wd{i}", [128, D], BF16) for i in range(NB)]
    itT = sb("itT", [128, NB * TB], BF16)
    gel = sb("gel", [128, TB], BF16)
    oacc = sb("oacc", [128, 4 * D], F32)
    res = [sb(f"res{i}", [128, D], F32) for i in range(2)]
    stats = sb("stats", [128, 8], F32)
    s1c, s2c, mc, m2c, vc, rstdc, nbc = [stats[:, i:i + 1] for i in range(7)]
    hb2 = sb("hb2", [128, D], F32) if b2ap is not None else None
    b1t = sb("b1t", [128, nt_all], F32) if b1ap is not None else None
    b2row = sb("b2row", [1, D], F32) if b2ap is not None else None

    pg = [b.psum(ns + f"pg{i}", [128, TB], F32, sc) for i in range(2)]
    pu = [b.psum(ns + f"pu{i}", [128, TB], F32, sc) for i in range(2)]
    pd = [b.psum(ns + f"pd{i}", [128, 512], F32, sc) for i in range(2)]
    ptr = [b.psum(ns + f"ptr{i}", [128, 128], BF16, sc) for i in range(2)]

    LDXp = [b.sem(ns + f"LDX{i}") for i in range(2)]
    STXp = [b.sem(ns + f"STXs{i}") for i in range(2)]
    WGp = [b.sem(ns + f"WG{i}") for i in range(2)]
    WUp = [b.sem(ns + f"WU{i}") for i in range(2)]
    WD = b.sem(ns + "WDs")
    LNV = b.sem(ns + "LNV")
    LNS = b.sem(ns + "LNS")
    TRD = b.sem(ns + "TRD")
    TRE = b.sem(ns + "TRE")
    WLD = b.sem(ns + "WLD")
    MMG = b.sem(ns + "MMG")
    MMU = b.sem(ns + "MMU")
    MMD = b.sem(ns + "MMD")
    GEL = b.sem(ns + "GEL")
    INT = b.sem(ns + "INT")
    EVD = b.sem(ns + "EVD")
    RES = b.sem(ns + "RES")
    STX = b.sem(ns + "STX")
    BIAD = b.sem(ns + "BIAD")
    BIAC = b.sem(ns + "BIAC")

    if b1ap is not None:
        for r in range(nt_all):
            b.dma("gpsimd", b1t[:, r:r + 1],
                  b1ap[r * 128:(r + 1) * 128].rearrange("(p o) -> p o", o=1), inc=BIAD)
    hb2_done = 0
    if b2ap is not None:
        b.dma("gpsimd", b2row[:], b2ap.rearrange("(o d) -> o d", o=1), inc=BIAD)
        bia0 = BIAD.n
        for q in range(4):
            b.on("tensor", (lambda e, qq=q: e.matmul(pg[0][:], C["ones_1x128f"][0:1, :],
                                                     b2row[0:1, qq * 512:(qq + 1) * 512],
                                                     start=True, stop=True)),
                 wait=[BIAD.at(bia0), C["_cld"]] + ([BIAC.at(BIAC.n)] if q > 0 else []), inc=BIAC)
            b.on("vector", (lambda e, qq=q: e.tensor_scalar(hb2[:, qq * 512:(qq + 1) * 512], pg[0][:],
                                                            0.5, None, ALU.mult)),
                 wait=[BIAC.at(BIAC.n)], inc=BIAC)
        hb2_done = BIAC.n
    bia_done = BIAD.n

    gelu_f = AF.Gelu if is_l1 else AF.Gelu_apprx_tanh

    mmg_tb = mmu_tb = 0
    lns_hist, trd_hist, gel_hist, int_hist, mmd_hist, stx_hist, wld_hist = ({} for _ in range(7))

    for tb in range(debug_tb):
        t0 = tb * TB
        for i in range(4):
            ti = tb * 4 + i
            w = []
            if ti >= 2:
                w.append(LNV.at(lns_hist[ti - 2][1]))
            if tb > 0:
                w.append(STXp[i % 2].at(stx_hist[("res", (tb - 1) * 4 + i)]))
            LDc = LDXp[i % 2]
            b.dma("gpsimd", xt[i % 2][:], stream_in[t0 + i * 128: t0 + (i + 1) * 128, :],
                  wait=w, inc=LDc)
            ldv = LDc.n
            b.on("vector", (lambda e, x=xt[i % 2]: e.tensor_reduce(s1c, x[:], AX.X, ALU.add)),
                 wait=[LDc.at(ldv)], inc=LNV)
            b.on("scalar", (lambda e, x=xt[i % 2]: e.activation(sq[:], x[:], AF.Square)),
                 wait=[LDc.at(ldv)] + ([LNV.at(lns_hist[("s2", ti - 1)])] if ti > 0 else []),
                 inc=LNS)
            b.on("vector", (lambda e: e.tensor_reduce(s2c, sq[:], AX.X, ALU.add)),
                 wait=[LNS.at(LNS.n)], inc=LNV)
            lns_hist[("s2", ti)] = LNV.n
            b.on("vector", lambda e: e.tensor_scalar(mc, s1c, 1.0 / D, None, ALU.mult),
                 wait=[LNV.at(LNV.n)], inc=LNV)
            b.on("vector", lambda e: e.tensor_tensor(m2c, mc, mc, ALU.mult),
                 wait=[LNV.at(LNV.n)], inc=LNV)
            b.on("vector", lambda e: e.scalar_tensor_tensor(vc, s2c, 1.0 / D, m2c, ALU.mult, ALU.subtract),
                 wait=[LNV.at(LNV.n)], inc=LNV)
            vv = LNV.n
            b.on("scalar", (lambda e: e.activation(vc, vc, AF.Ln, bias=fc[EPS])),
                 wait=[LNV.at(vv), fc["_sem"]], inc=LNS)
            b.on("scalar", (lambda e: e.activation(rstdc, vc, AF.Exp, scale=fc[-0.5])), inc=LNS)
            sv = LNS.n
            b.on("vector", (lambda e: e.scalar_tensor_tensor(nbc, mc, -1.0, rstdc, ALU.mult, ALU.mult)),
                 wait=[LNS.at(sv)], inc=LNV)
            nv = LNV.n
            b.on("vector", (lambda e, x=xt[i % 2]: e.tensor_scalar(xhat[:], x[:],
                                                                   rstdc, nbc, ALU.mult, ALU.add)),
                 wait=[LNV.at(nv)] + ([TRD.at(trd_hist[ti - 1])] if ti > 0 else []),
                 inc=LNV)
            lns_hist[ti] = ("LNV", LNV.n)
            xv = LNV.n
            for r in range(16):
                tix = TRD.n
                wpe = [LNV.at(xv)]
                if tix >= 2:
                    wpe.append(TRE.at(tix - 1))
                if tb > 0 and i == 0 and r == 0:
                    wpe += [MMG.at(mmg_tb)] + ([MMU.at(mmu_tb)] if not is_l1 else [])
                if tb == 0 and ti == 0 and r == 0:
                    wpe.append(C["_cld"])
                b.on("tensor", (lambda e, rr=r, p=ptr[tix % 2]:
                                e.transpose(p[:], xhat[:, rr * 128:(rr + 1) * 128], C["ident"][:])),
                     wait=wpe, inc=TRD)
                b.on("scalar", (lambda e, rr=r, ii=i, p=ptr[tix % 2]:
                                e.copy(xhT[:, rr * TB + ii * 128: rr * TB + (ii + 1) * 128], p[:])),
                     wait=[TRD.at(TRD.n)], inc=TRE)
            trd_hist[ti] = TRD.n
        tre_tb = TRE.n
        if debug_stage == "ln":
            for i in range(4):
                b.dma("gpsimd", stream_out[t0 + i * 128: t0 + (i + 1) * 128, 0:D],
                      xt[i % 2][:], wait=[TRE.at(tre_tb)], inc=STXp[i % 2])
                stx_hist[("res", tb * 4 + i)] = STXp[i % 2].n
            continue

        mmg_tb0, mmu_tb0 = MMG.n, MMU.n
        for nt in range(nt_all):
            bund = nt // NB
            wldw = []
            if nt >= 2:
                wldw.append(MMG.at(mmg_tb0 + nt - 1))
            elif tb > 0:
                wldw.append(MMG.at(mmg_tb0))
            WGc = WGp[nt % 2]
            b.dma("sync", wg[nt % 2][:], gph[nt], wait=wldw, inc=WGc)
            wg_ld = WGc.n
            if not is_l1:
                wldu = []
                if nt >= 2:
                    wldu.append(MMU.at(mmu_tb0 + nt - 1))
                elif tb > 0:
                    wldu.append(MMU.at(mmu_tb0))
                WUc = WUp[nt % 2]
                b.dma("sync", wu[nt % 2][:], uph[nt], wait=wldu, inc=WUc)
                wu_ld = WUc.n
            wldd = []
            if bund >= 1:
                wldd.append(MMD.at(mmd_hist[(tb, bund - 1)]))
            elif tb > 0:
                wldd.append(MMD.at(mmd_hist[(tb - 1, nbund - 1)]))
            b.dma("sync", wd[nt % NB][:], dph[nt], wait=wldd, inc=WD)
            wld_hist[nt] = WD.n

            wmm = [WGc.at(wg_ld), TRE.at(tre_tb)]
            if nt >= 2:
                wmm.append(GEL.at(gel_hist[(tb, nt - 2)]))
            elif tb > 0:
                wmm.append(GEL.at(gel_hist[(tb - 1, nt_all - 2 + nt)]))
            if tb == 0 and nt == 0:
                wmm.append(C["_cld"])
                if b2ap is not None:
                    wmm.append(BIAC.at(hb2_done))
            for r in range(16):
                last = (r == 15)
                b.on("tensor", (lambda e, rr=r, n_=nt: e.matmul(
                    pg[n_ % 2][:], wg[n_ % 2][:, rr * 128:(rr + 1) * 128],
                    xhT[:, rr * TB:(rr + 1) * TB], start=(rr == 0), stop=(rr == 15))),
                    wait=(wmm if r == 0 else ()), inc=(MMG if last else None))
            if not is_l1:
                wmu = [WUc.at(wu_ld)]
                if nt >= 2:
                    wmu.append(INT.at(int_hist[(tb, nt - 2)]))
                elif tb > 0:
                    wmu.append(INT.at(int_hist[(tb - 1, nt_all - 2 + nt)]))
                for r in range(16):
                    last = (r == 15)
                    b.on("tensor", (lambda e, rr=r, n_=nt: e.matmul(
                        pu[n_ % 2][:], wu[n_ % 2][:, rr * 128:(rr + 1) * 128],
                        xhT[:, rr * TB:(rr + 1) * TB], start=(rr == 0), stop=(rr == 15))),
                        wait=(wmu if r == 0 else ()), inc=(MMU if last else None))
            wg_ = [MMG.at(MMG.n)]
            if bund >= 1:
                wg_.append(MMD.at(mmd_hist[(tb, bund - 1)]))
            elif tb > 0:
                wg_.append(MMD.at(mmd_hist[(tb - 1, nbund - 1)]))
            if is_l1:
                b.on("scalar", (lambda e, n_=nt: e.activation(
                    itT[:, (n_ % NB) * TB:((n_ % NB) + 1) * TB], pg[n_ % 2][:], gelu_f,
                    bias=b1t[:, n_:n_ + 1])),
                    wait=wg_ + [BIAD.at(bia_done)], inc=GEL)
                gel_hist[(tb, nt)] = GEL.n
                int_hist[(tb, nt)] = INT.n
            else:
                prev_int = int_hist.get((tb, nt - 1), int_hist.get((tb - 1, nt_all - 1), 0)) \
                    if (nt > 0 or tb > 0) else 0
                b.on("scalar", (lambda e, n_=nt: e.activation(gel[:], pg[n_ % 2][:], gelu_f)),
                     wait=wg_ + ([INT.at(prev_int)] if prev_int else []), inc=GEL)
                gel_hist[(tb, nt)] = GEL.n
                wi_ = [GEL.at(GEL.n), MMU.at(MMU.n)]
                if bund >= 1:
                    wi_.append(MMD.at(mmd_hist[(tb, bund - 1)]))
                elif tb > 0:
                    wi_.append(MMD.at(mmd_hist[(tb - 1, nbund - 1)]))
                b.on("vector", (lambda e, n_=nt: e.tensor_tensor(
                    itT[:, (n_ % NB) * TB:((n_ % NB) + 1) * TB], gel[:], pu[n_ % 2][:], ALU.mult)),
                    wait=wi_, inc=INT)
                int_hist[(tb, nt)] = INT.n

            if debug_stage == "inter" and tb == 0 and nt == 0:
                dbgf = b.sbuf(ns + "dbgf", [128, TB], F32, sc)
                b.on("vector", (lambda e: e.tensor_copy(dbgf[:], itT[:, 0:TB])),
                     wait=[INT.at(INT.n), GEL.at(GEL.n)], inc=INT)
                b.dma("gpsimd", stream_out[1024:1152, 0:TB], dbgf[:], wait=[INT.at(INT.n)], inc=STX)
                dbg2 = b.sbuf(ns + "dbg2", [128, TB], F32, sc)
                b.on("vector", (lambda e: e.tensor_copy(dbg2[:], gel[:])),
                     wait=[INT.at(INT.n)], inc=INT)
                b.dma("gpsimd", stream_out[1152:1280, 0:TB], dbg2[:], wait=[INT.at(INT.n)], inc=STX)
                dbg3 = b.sbuf(ns + "dbg3", [128, TB], F32, sc)
                b.on("vector", (lambda e: e.tensor_copy(dbg3[:], xhT[:, 0:TB])),
                     wait=[INT.at(INT.n)], inc=INT)
                b.dma("gpsimd", stream_out[1280:1408, 0:TB], dbg3[:], wait=[INT.at(INT.n)], inc=STX)
            if debug_stage == "gate":
                mmd_hist[(tb, nt // NB)] = MMD.n
                continue
            if nt % NB == NB - 1:
                for tsub in range(4):
                    for ds in range(4):
                        g = MMD.n
                        wdm = [(INT if not is_l1 else GEL).at(
                            (int_hist if not is_l1 else gel_hist)[(tb, nt)]),
                            WD.at(wld_hist[nt])]
                        if g >= 2:
                            wdm.append(EVD.at(g - 1))
                        for j in range(NB):
                            last = (j == NB - 1)
                            b.on("tensor", (lambda e, jj=j, t_=tsub, d_=ds, p=pd[g % 2]:
                                            e.matmul(p[:],
                                                     itT[:, jj * TB + t_ * 128: jj * TB + (t_ + 1) * 128],
                                                     wd[jj][:, d_ * 512:(d_ + 1) * 512],
                                                     start=(jj == 0), stop=(jj == NB - 1))),
                                 wait=(wdm if j == 0 else ()), inc=(MMD if last else None))
                        if bund == 0:
                            b.on("vector", (lambda e, t_=tsub, d_=ds, p=pd[g % 2]:
                                            e.tensor_copy(oacc[:, t_ * D + d_ * 512: t_ * D + (d_ + 1) * 512],
                                                          p[:])),
                                 wait=[MMD.at(MMD.n)], inc=EVD)
                        else:
                            b.on("vector", (lambda e, t_=tsub, d_=ds, p=pd[g % 2]:
                                            e.tensor_tensor(oacc[:, t_ * D + d_ * 512: t_ * D + (d_ + 1) * 512],
                                                            oacc[:, t_ * D + d_ * 512: t_ * D + (d_ + 1) * 512],
                                                            p[:], ALU.add)),
                                 wait=[MMD.at(MMD.n)], inc=EVD)
                mmd_hist[(tb, bund)] = MMD.n
        mmg_tb, mmu_tb = MMG.n, MMU.n

        if debug_stage == "gate":
            for i in range(4):
                b.dma("gpsimd", stream_out[t0 + i * 128: t0 + (i + 1) * 128, 0:D],
                      xt[i % 2][:], wait=[INT.at(INT.n), GEL.at(GEL.n)], inc=STXp[i % 2])
                stx_hist[("res", tb * 4 + i)] = STXp[i % 2].n
            mmg_tb, mmu_tb = MMG.n, MMU.n
            continue
        evd_tb = EVD.n
        for i in range(4):
            ti = tb * 4 + i
            w = [EVD.at(evd_tb), LNV.at(lns_hist[ti][1])]
            if ti >= 2:
                w.append(STXp[i % 2].at(stx_hist[("res", ti - 2)]))
            LDc = LDXp[i % 2]
            b.dma("gpsimd", xt[i % 2][:], stream_in[t0 + i * 128: t0 + (i + 1) * 128, :],
                  wait=w, inc=LDc)
            ldv = LDc.n
            if b2ap is not None:
                b.on("vector", (lambda e, ii=i: e.tensor_tensor(
                    res[ii % 2][:], oacc[:, ii * D:(ii + 1) * D], hb2[:], ALU.add)),
                    wait=[LDc.at(ldv), BIAC.at(hb2_done)] +
                    ([STXp[i % 2].at(stx_hist[("res", ti - 2)])] if ti >= 2 else []), inc=RES)
                b.on("vector", (lambda e, ii=i, x=xt[i % 2]: e.tensor_tensor(
                    res[ii % 2][:], res[ii % 2][:], x[:], ALU.add)),
                    wait=[RES.at(RES.n)], inc=RES)
            else:
                b.on("vector", (lambda e, ii=i, x=xt[i % 2]: e.tensor_tensor(
                    res[ii % 2][:], oacc[:, ii * D:(ii + 1) * D], x[:], ALU.add)),
                    wait=[LDc.at(ldv)] +
                    ([STXp[i % 2].at(stx_hist[("res", ti - 2)])] if ti >= 2 else []), inc=RES)
            b.dma("gpsimd", stream_out[t0 + i * 128: t0 + (i + 1) * 128, :], res[i % 2][:],
                  wait=[RES.at(RES.n)], inc=STXp[i % 2])
            stx_hist[("res", ti)] = STXp[i % 2].n

    sc.close()
    return [(STXp[0], STXp[0].n), (STXp[1], STXp[1].n)]


def build_ffn(nmlp, with_bias, debug_stage="full", debug_tb=4):
    nc = bass.Bass("TRN2", target_bir_lowering=False)
    stack = ExitStack()
    with stack:
        b = B(nc, stack)
        names = ["ident", "ones_1x128b", "ones_1x128f"]
        C, cld = load_consts(b, names)
        C["_cld"] = cld
        fc, fcs = make_fc(b, [EPS, -0.5, 0.5])
        fc["_sem"] = fcs
        nt_all = nmlp // 128
        sin = b.dram("sin", [S, D], F32, "ExternalInput")
        sout = b.dram("sout", [S, D], F32, "ExternalOutput")
        gph = b.dram("gph", [nt_all, 128, D], BF16, "ExternalInput")
        uph = b.dram("uph", [nt_all, 128, D], BF16, "ExternalInput") if nmlp == MLP0 else None
        dph = b.dram("dph", [nt_all, 128, D], BF16, "ExternalInput")
        b1ap = b.dram("b1", [nmlp], F32, "ExternalInput") if with_bias else None
        b2ap = b.dram("b2", [D], F32, "ExternalInput") if with_bias else None
        fw = emit_ffn(b, C, fc, "f_", sin, sout, nmlp, gph, uph, dph, b1ap, b2ap,
                      debug_stage=debug_stage, debug_tb=debug_tb)
        b.finalize(final_waits=fw)
    return nc


def emit_attn(b, C, fc, ns, stream_in, stream_out, layer, qph, kph, vph, oph,
              qk_sp=None, at_sp=None, bqap=None, bkap=None, bvap=None, boap=None,
              dbg_aps=None):
    NQT = 16
    is0 = (layer == 0)
    cq_t = C["cq0" if is0 else "cq1"]
    sq_t = C["sq0" if is0 else "sq1"]
    ck_t = C["ck0" if is0 else "ck1"]
    sk_t = C["sk0" if is0 else "sk1"]

    outer = ExitStack()

    def sbo(nm, shape, dt, scope=None):
        return b.sbuf(ns + nm, shape, dt, scope or outer)

    negmrow = sbo("negmrow", [1, S], BF16)
    negm_allh = sbo("negm_allh", [128, 256], F32)
    stats = sbo("stats", [128, 8], F32)
    s1c, s2c, mc, m2c, vc, rstdc, nbc = [stats[:, i:i + 1] for i in range(7)]
    colsums = sbo("colsums", [16, S], BF16) if is0 else None
    r1b = sbo("r1b", [128, 1], BF16) if is0 else None
    csrow = sbo("csrow", [1, S], BF16) if is0 else None
    hnT = sbo("hnT", [128, 256], BF16) if is0 else None
    bis = sbo("bis", [16, 8], F32) if is0 else None
    csz = sbo("csz", [1, 512], BF16) if is0 else None
    scX = ExitStack()
    outer.enter_context(scX)
    xhT = sbo("xhT", [128, 16 * S], BF16, scX)

    nm_ = ["LDX", "LNV", "LNS", "TRD", "TRE", "WLD", "PRJ", "RPE", "SMM", "ADV",
           "AEX", "CSM", "NTP", "NTE", "SPL", "BIS", "BLD", "BMM", "BDV", "BEX",
           "ZPV", "EVC", "APW", "OPL", "OMM", "ORS", "OST", "VMM", "VEV", "BIAD",
           "BIAC", "LDA0", "LDA1", "WQ0", "WQ1", "VG", "OPN", "AP0", "AP1", "X30",
           "X31", "OS0", "OS1", "BVR", "BCL", "BOR"]
    S_ = {n: b.sem(ns + n) for n in nm_}
    LDXv, LNVv, LNSv, TRDv, TREv = S_["LDX"], S_["LNV"], S_["LNS"], S_["TRD"], S_["TRE"]
    WLD, PRJ, RPE, SMM, ADV = S_["WLD"], S_["PRJ"], S_["RPE"], S_["SMM"], S_["ADV"]
    AEX, CSM, NTP, NTE, SPL = S_["AEX"], S_["CSM"], S_["NTP"], S_["NTE"], S_["SPL"]
    BIS, BLD, BMM, BDV, BEX = S_["BIS"], S_["BLD"], S_["BMM"], S_["BDV"], S_["BEX"]
    ZPV, EVC, APW, OPL, OMM = S_["ZPV"], S_["EVC"], S_["APW"], S_["OPL"], S_["OMM"]
    ORS, OST, VMM, VEV = S_["ORS"], S_["OST"], S_["VMM"], S_["VEV"]
    BIAD, BIAC = S_["BIAD"], S_["BIAC"]
    BVR, BCL, BOR = S_["BVR"], S_["BCL"], S_["BOR"]
    LDAp = [S_["LDA0"], S_["LDA1"]]
    WQp = [S_["WQ0"], S_["WQ1"]]
    VG, OPN = S_["VG"], S_["OPN"]
    APp = [S_["AP0"], S_["AP1"]]
    X3p = [S_["X30"], S_["X31"]]
    OSp = [S_["OS0"], S_["OS1"]]

    hist = {}
    CSE = b.sem(ns + "CSE")
    CSD = b.sem(ns + "CSD")
    ZRW = None
    if is0:
        ZRW = b.sem(ns + "ZRW")
        b.on("vector", lambda e: e.memset(csz[:], 0.0), inc=ZRW)

    sc1 = ExitStack()
    outer.enter_context(sc1)
    xt1 = [sbo(f"xt{i}", [128, D], F32, sc1) for i in range(2)]
    sq1_ = sbo("sq", [128, D], F32, sc1)
    xhat = sbo("xhat", [128, D], BF16, sc1)
    psc1 = ExitStack()
    outer.enter_context(psc1)
    ptr = [psc1.enter_context(b.nc.psum_tensor(ns + f"ptr{i}", [128, 128], BF16))
           for i in range(2)]
    NLN = int(os.environ.get("K_NLN", "16"))
    NOTR = os.environ.get("K_NOTR") == "1"
    lnh = {}
    for ti in range(NLN):
        w = []
        if ti >= 2:
            w.append(LNVv.at(lnh[ti - 2]))
        LDc = LDAp[ti % 2]
        b.dma("gpsimd", xt1[ti % 2][:], stream_in[ti * 128:(ti + 1) * 128, :], wait=w, inc=LDc)
        ldv = LDc.n
        b.on("vector", (lambda e, x=xt1[ti % 2]: e.tensor_reduce(s1c, x[:], AX.X, ALU.add)),
             wait=[LDc.at(ldv)], inc=LNVv)
        b.on("scalar", (lambda e, x=xt1[ti % 2]: e.activation(sq1_[:], x[:], AF.Square)),
             wait=[LDc.at(ldv)] + ([LNVv.at(lnh[("s2", ti - 1)])] if ti > 0 else []), inc=LNSv)
        b.on("vector", (lambda e: e.tensor_reduce(s2c, sq1_[:], AX.X, ALU.add)),
             wait=[LNSv.at(LNSv.n)], inc=LNVv)
        lnh[("s2", ti)] = LNVv.n
        b.on("vector", lambda e: e.tensor_scalar(mc, s1c, 1.0 / D, None, ALU.mult),
             wait=[LNVv.at(LNVv.n)], inc=LNVv)
        b.on("vector", lambda e: e.tensor_tensor(m2c, mc, mc, ALU.mult),
             wait=[LNVv.at(LNVv.n)], inc=LNVv)
        b.on("vector", lambda e: e.scalar_tensor_tensor(vc, s2c, 1.0 / D, m2c, ALU.mult, ALU.subtract),
             wait=[LNVv.at(LNVv.n)], inc=LNVv)
        b.on("scalar", (lambda e: e.activation(vc, vc, AF.Ln, bias=fc[EPS])),
             wait=[LNVv.at(LNVv.n), fc["_sem"]], inc=LNSv)
        b.on("scalar", (lambda e: e.activation(rstdc, vc, AF.Exp, scale=fc[-0.5])), inc=LNSv)
        b.on("vector", (lambda e: e.scalar_tensor_tensor(nbc, mc, -1.0, rstdc, ALU.mult, ALU.mult)),
             wait=[LNSv.at(LNSv.n)], inc=LNVv)
        b.on("vector", (lambda e, x=xt1[ti % 2]: e.tensor_scalar(xhat[:], x[:], rstdc, nbc,
                                                                 ALU.mult, ALU.add)),
             wait=[LNVv.at(LNVv.n)] + ([TRDv.at(lnh[("tr", ti - 1)])] if (ti > 0 and not NOTR) else []),
             inc=LNVv)
        lnh[ti] = LNVv.n
        xv = LNVv.n
        for r in range(0 if NOTR else 16):
            tix = TRDv.n
            wpe = [LNVv.at(xv)]
            if tix >= 2:
                wpe.append(TREv.at(tix - 1))
            if tix == 0:
                wpe.append(C["_cld"])
            b.on("tensor", (lambda e, rr=r, p=ptr[tix % 2]:
                            e.transpose(p[:], xhat[:, rr * 128:(rr + 1) * 128], C["ident"][:])),
                 wait=wpe, inc=TRDv)
            b.on("scalar", (lambda e, rr=r, tt=ti, p=ptr[tix % 2]:
                            e.copy(xhT[:, rr * S + tt * 128: rr * S + (tt + 1) * 128], p[:])),
                 wait=[TRDv.at(TRDv.n)], inc=TREv)
        lnh[("tr", ti)] = TRDv.n if not NOTR else 0
    tre_all = TREv.n
    ln_guard = [TREv.at(tre_all), LNVv.at(LNVv.n)]
    sc1.close()
    psc1.close()

    PS = b.psum(ns + "PS", [128, 2048], F32, outer)
    PC = b.psum(ns + "PC", [128, 2048], F32, outer)
    PSq = [PS[:, i * 512:(i + 1) * 512] for i in range(4)]
    PCq = [PC[:, i * 512:(i + 1) * 512] for i in range(4)]

    sc2 = ExitStack()
    outer.enter_context(sc2)
    wq = [sbo(f"wq{i}", [128, D], BF16, sc2) for i in range(2)]
    qs = sbo("qs", [128, 512], F32, sc2)
    rt1 = sbo("rt1", [128, 512], F32, sc2)
    rt2 = sbo("rt2", [128, 512], F32, sc2)
    qr = sbo("qr", [128, S], BF16, sc2)
    kr = [sbo(f"kr{g}", [128, S], BF16, sc2) for g in (range(1) if is0 else range(4))]
    E1 = sbo("E1", [128, S], BF16, sc2) if is0 else None
    zr1 = sbo("zr1", [128, 4], F32, sc2)
    zrow = sbo("zrow", [1, 512], F32, sc2)
    v4 = sbo("v4", [128, 16 * 512], BF16, sc2)
    vg = sbo("vg", [128, MLP1], BF16, sc2)
    e2t = [sbo(f"e2t{i}", [128, 512], BF16, sc2) for i in range(2)]
    s2a = [sbo(f"s2a{i}", [128, 512], F32, sc2) for i in range(2)]
    s2b = [sbo(f"s2b{i}", [128, 512], F32, sc2) for i in range(2)]
    atn = sbo("atn", [128, 512], BF16, sc2)
    r2row = sbo("r2row", [1, 512], BF16, sc2)
    r2b = sbo("r2b", [128, 512], F32, sc2)
    bcol = sbo("bcol", [128, 2], F32, sc2)
    bvb = sbo("bvb", [128, 512], F32, sc2) if bvap is not None else None

    if bvap is not None and os.environ.get("K_NOBV") != "1":
        bvrow = sbo("bvrow", [1, 512], F32, sc2)
        b.dma("gpsimd", bvrow[:], bvap.rearrange("(o d) -> o d", o=1), wait=ln_guard, inc=BVR)
        b.on("tensor", lambda e: e.matmul(PCq[0], C["ones_1x128f"][0:1, :], bvrow[:],
                                          start=True, stop=True),
             wait=[BVR.at(BVR.n), C["_cld"]], inc=BIAC)
        b.on("vector", lambda e: e.tensor_copy(bvb[:], PCq[0]), wait=[BIAC.at(BIAC.n)], inc=BIAC)
    bia_bv = BIAC.n

    def rope_apply(dst_slice, psrc, c_sl, s_sl, biascol, wait0):
        if biascol is None:
            b.on("vector", (lambda e, p=psrc: e.tensor_copy(qs[:], p)), wait=wait0, inc=RPE)
        else:
            b.on("vector", (lambda e, p=psrc, bc=biascol: e.tensor_scalar(qs[:], p, bc, None, ALU.add)),
                 wait=wait0, inc=RPE)
        qv = RPE.n
        b.on("vector", (lambda e, c=c_sl: e.tensor_tensor(rt1[:], qs[:], c, ALU.mult)),
             wait=[RPE.at(qv), C["_cld"]], inc=RPE)
        b.on("vector", (lambda e: e.tensor_copy(rt2[0:64, :], qs[64:128, :])), inc=RPE)
        b.on("vector", (lambda e: e.tensor_copy(rt2[64:128, :], qs[0:64, :])), inc=RPE)
        b.on("vector", (lambda e, s_=s_sl: e.tensor_tensor(rt2[:], rt2[:], s_[:], ALU.mult)),
             wait=[RPE.at(RPE.n)], inc=RPE)
        b.on("vector", (lambda e, d=dst_slice: e.tensor_tensor(d, rt1[:], rt2[:], ALU.add)),
             wait=[RPE.at(RPE.n)], inc=RPE)
        return RPE.n

    def proj_head(dst, pan_ap, c_t, s_t, biascol, qref, guard):
        wbuf = wq[qref % 2]
        WQc = WQp[qref % 2]
        wld_wait = list(guard.get("load", []))
        if qref >= 2:
            wld_wait.append(PRJ.at(hist[("prj", qref - 2)]))
        else:
            wld_wait.extend(ln_guard)
        b.dma("sync", wbuf[:], pan_ap, wait=wld_wait, inc=WQc)
        wv = WQc.n
        rv0 = RPE.n
        rv = rv0
        evacs = {}
        for ts in range(4):
            pq = PSq[ts % 2]
            wmm = [WQc.at(wv), TREv.at(tre_all)]
            if ts >= 2:
                wmm.append(RPE.at(evacs[ts - 2]))
            else:
                wmm.append(RPE.at(rv0))
                wmm.extend(guard.get("mm", []))
            for r in range(16):
                b.on("tensor", (lambda e, rr=r, p=pq, t_=ts: e.matmul(
                    p, wbuf[:, rr * 128:(rr + 1) * 128],
                    xhT[:, rr * S + t_ * 512: rr * S + t_ * 512 + 512],
                    start=(rr == 0), stop=(rr == 15))),
                    wait=(wmm if r == 0 else ()), inc=(PRJ if r == 15 else None))
            prjv = PRJ.n
            w0 = [PRJ.at(prjv)] + list(guard.get("rope0", []) if ts == 0 else [])
            evacs[ts] = RPE.n + 1
            rv = rope_apply(dst[:, ts * 512:(ts + 1) * 512], pq,
                            c_t[:, ts * 512:(ts + 1) * 512], s_t[:, ts * 512:(ts + 1) * 512],
                            biascol, w0)
        hist[("prj", qref)] = PRJ.n
        return rv

    def compute_vgroup(g, gi, guard):
        b.dma("sync", vg[:], vph[g], wait=list(guard.get("vload", [])) +
              ([] if gi > 0 else ln_guard), inc=VG)
        wv = VG.n
        for tt in range(16):
            pq = PSq[2 + tt % 2]
            wmm = [VG.at(wv), TREv.at(tre_all)]
            if tt >= 2:
                wmm.append(VEV.at(VEV.n - 1))
            else:
                wmm.extend(guard.get("vmm", []))
            for r in range(16):
                b.on("tensor", (lambda e, rr=r, p=pq, t_=tt: e.matmul(
                    p, xhT[:, rr * S + t_ * 128: rr * S + (t_ + 1) * 128],
                    vg[:, rr * 512:(rr + 1) * 512],
                    start=(rr == 0), stop=(rr == 15))),
                    wait=(wmm if r == 0 else ()), inc=(VMM if r == 15 else None))
            if bvap is not None:
                b.on("vector", (lambda e, t_=tt, p=pq: e.tensor_tensor(
                    v4[:, t_ * 512:(t_ + 1) * 512], p, bvb[:], ALU.add)),
                    wait=[VMM.at(VMM.n), BIAC.at(bia_bv)], inc=VEV)
            else:
                b.on("vector", (lambda e, t_=tt, p=pq: e.tensor_copy(
                    v4[:, t_ * 512:(t_ + 1) * 512], p)),
                    wait=[VMM.at(VMM.n)], inc=VEV)
        hist[("vev_cur",)] = VEV.n
        hist[("vmm_cur",)] = VMM.n

    def passA_L0(h):
        for qt in range(NQT):
            njs = qt // 4 + 1
            wmm = [RPE.at(hist[("rope_k", h)]),
                   AEX.at(hist.get(("exp", h, qt - 1), hist.get(("exp", h - 1, 15), 0))),
                   ADV.at(hist.get(("negmv", h, qt - 1), hist.get(("negmv", h - 1, 15), 0))),
                   CSM.at(hist.get(("cs", h, qt - 1), hist.get(("cs", h - 1, 15), 0)))]
            for js in range(njs):
                b.on("tensor", (lambda e, j=js, q=qt: e.matmul(
                    PS[:, j * 512:(j + 1) * 512], qr[:, q * 128:(q + 1) * 128],
                    kr[0][:, j * 512:(j + 1) * 512], start=True, stop=True)),
                    wait=(wmm if js == 0 else ()), inc=(SMM if js == njs - 1 else None))
            sv = SMM.n
            b.on("vector", (lambda e, q=qt: e.tensor_tensor(
                PS[:, q * 128:(q + 1) * 128], PS[:, q * 128:(q + 1) * 128],
                C["cpA"][:], ALU.add)), wait=[SMM.at(sv), C["_cld"]], inc=ADV)
            b.on("vector", (lambda e, q=qt, hh=h: e.tensor_reduce(
                negm_allh[:, hh * 16 + q: hh * 16 + q + 1], PS[:, 0:(q + 1) * 128],
                AX.X, ALU.max, negate=True)),
                wait=[ADV.at(ADV.n)], inc=ADV)
            hist[("negmv", h, qt)] = ADV.n
            b.on("scalar", (lambda e, q=qt, hh=h: e.activation(
                E1[:, 0:(q + 1) * 128], PS[:, 0:(q + 1) * 128], AF.Exp,
                bias=negm_allh[:, hh * 16 + q: hh * 16 + q + 1])),
                wait=[ADV.at(ADV.n),
                      CSM.at(hist.get(("cs", h, qt - 1), hist.get(("cs", h - 1, 15), 0)))],
                inc=AEX)
            hist[("exp", h, qt)] = AEX.n
            b.on("vector", (lambda e, q=qt: e.tensor_reduce(
                zr1[:, 0:1], E1[:, 0:(q + 1) * 128], AX.X, ALU.add)),
                wait=[AEX.at(AEX.n)], inc=ADV)
            b.on("vector", (lambda e: e.reciprocal(zr1[:, 1:2], zr1[:, 0:1])),
                 wait=[ADV.at(ADV.n)], inc=ADV)
            wr1 = [ADV.at(ADV.n),
                   CSM.at(hist.get(("cs", h, qt - 1), hist.get(("cs", h - 1, 15), 0)))]
            if qt == 0:
                wr1.append(CSE.at(CSE.n))
            b.on("vector", (lambda e: e.tensor_copy(r1b[:], zr1[:, 1:2])), wait=wr1, inc=ADV)
            rv = ADV.n
            for js in range(njs):
                st_ = (qt == 4 * js)
                sp_ = (qt == 15)
                wjs = min(512, (qt + 1) * 128 - js * 512)
                w0 = [ADV.at(rv)] if js == 0 else []
                if st_:
                    if h == 0 and qt == 0:
                        w0 = w0 + [ZRW.at(ZRW.n)]
                    b.on("tensor", (lambda e, j=js: e.matmul(
                        PC[0:1, j * 512:(j + 1) * 512], C["ones_1x128b"][0:1, 0:1],
                        csz[:], start=True, stop=False)), wait=w0)
                    w0 = []
                b.on("tensor", (lambda e, j=js, s1=sp_, w_=wjs: e.matmul(
                    PC[0:1, j * 512: j * 512 + w_], r1b[:], E1[:, j * 512: j * 512 + w_],
                    start=False, stop=s1)),
                    wait=w0, inc=(CSM if js == njs - 1 else None))
            hist[("cs", h, qt)] = CSM.n
        b.on("vector", (lambda e: e.tensor_copy(csrow[:], PC[0:1, :])),
             wait=[CSM.at(hist[("cs", h, 15)])] +
             ([CSD.at(hist[("csd", h - 1)])] if h > 0 else []), inc=CSE)
        b.dma("gpsimd", colsums[h:h + 1, :], csrow[:], wait=[CSE.at(CSE.n)], inc=CSD)
        hist[("csd", h)] = CSD.n

    def passA_L1(h, g):
        for qt in range(NQT):
            njs = qt // 4 + 1
            wmm = [RPE.at(hist[("rope_q", h)]),
                   ADV.at(hist.get(("negmv", h, qt - 1), hist.get(("pArb", h),  0)))]
            for js in range(njs):
                b.on("tensor", (lambda e, j=js, q=qt, g_=g: e.matmul(
                    PS[:, j * 512:(j + 1) * 512], qr[:, q * 128:(q + 1) * 128],
                    kr[g_][:, j * 512:(j + 1) * 512], start=True, stop=True)),
                    wait=(wmm if js == 0 else ()), inc=(SMM if js == njs - 1 else None))
            sv = SMM.n
            b.on("vector", (lambda e, q=qt: e.tensor_tensor(
                PS[:, q * 128:(q + 1) * 128], PS[:, q * 128:(q + 1) * 128],
                C["cpA"][:], ALU.add)), wait=[SMM.at(sv), C["_cld"]], inc=ADV)
            b.on("vector", (lambda e, q=qt, hh=h: e.tensor_reduce(
                negm_allh[:, hh * 16 + q: hh * 16 + q + 1], PS[:, 0:(q + 1) * 128],
                AX.X, ALU.max, negate=True)),
                wait=[ADV.at(ADV.n)], inc=ADV)
            hist[("negmv", h, qt)] = ADV.n

    def negm_to_row(h, guard=()):
        for qt in range(NQT):
            w = [ADV.at(hist[("negmv", h, 15)])] + (list(guard) if qt == 0 else [])
            if is0:
                w.append(AEX.at(hist[("exp", h, 15)]))
            if NTP.n >= 1:
                w.append(NTE.at(NTP.n))
            b.on("tensor", (lambda e, q=qt, hh=h: e.transpose(
                PS[0:1, 0:128], negm_allh[:, hh * 16 + q: hh * 16 + q + 1], C["identf"][:])),
                wait=w, inc=NTP)
            b.on("vector", (lambda e, q=qt: e.tensor_copy(
                negmrow[0:1, q * 128:(q + 1) * 128], PS[0:1, 0:128])),
                wait=[NTP.at(NTP.n)], inc=NTE)
        hist[("ntr", h)] = NTE.n

    def bisect():
        lo, hi, mid, cnt, sel, t1_, t2_ = [bis[:, i:i + 1] for i in range(7)]
        cmpb = sbo("cmpb", [16, S], BF16, sc2)
        b.on("vector", lambda e: e.tensor_copy(bis[:, 7:8], bis[:, 7:8]),
             wait=[CSD.at(hist[("csd", 15)])], inc=BIS)
        b.on("vector", lambda e: e.memset(lo, 0.0), inc=BIS)
        b.on("vector", lambda e: e.memset(hi, 2049.0), inc=BIS)
        b.on("vector", lambda e: e.memset(mid, 1024.5), inc=BIS)
        for it in range(45):
            b.on("vector", (lambda e: e.tensor_scalar(cmpb[:], colsums[:], mid, None, ALU.is_ge)),
                 wait=[BIS.at(BIS.n)], inc=BIS)
            b.on("vector", (lambda e: e.tensor_reduce(cnt, cmpb[:], AX.X, ALU.add)),
                 wait=[BIS.at(BIS.n)], inc=BIS)
            b.on("vector", (lambda e: e.tensor_scalar(sel, cnt, float(HEAVY) - 0.5, None, ALU.is_ge)),
                 wait=[BIS.at(BIS.n)], inc=BIS)
            b.on("vector", (lambda e: e.tensor_tensor(t1_, mid, lo, ALU.subtract)),
                 wait=[BIS.at(BIS.n)], inc=BIS)
            b.on("vector", (lambda e: e.tensor_tensor(t1_, sel, t1_, ALU.mult)),
                 wait=[BIS.at(BIS.n)], inc=BIS)
            b.on("vector", (lambda e: e.tensor_tensor(lo, lo, t1_, ALU.add)),
                 wait=[BIS.at(BIS.n)], inc=BIS)
            b.on("vector", (lambda e: e.tensor_tensor(t2_, hi, mid, ALU.subtract)),
                 wait=[BIS.at(BIS.n)], inc=BIS)
            b.on("vector", (lambda e: e.tensor_tensor(t2_, sel, t2_, ALU.mult)),
                 wait=[BIS.at(BIS.n)], inc=BIS)
            b.on("vector", (lambda e: e.tensor_tensor(hi, mid, t2_, ALU.add)),
                 wait=[BIS.at(BIS.n)], inc=BIS)
            b.on("vector", (lambda e: e.tensor_tensor(mid, lo, hi, ALU.add)),
                 wait=[BIS.at(BIS.n)], inc=BIS)
            b.on("vector", (lambda e: e.tensor_scalar(mid, mid, 0.5, None, ALU.mult)),
                 wait=[BIS.at(BIS.n)], inc=BIS)
        b.on("vector", (lambda e: e.tensor_scalar(cmpb[:], colsums[:], lo, None, ALU.is_ge)),
             wait=[BIS.at(BIS.n)], inc=BIS)
        cv = BIS.n
        for jt in range(16):
            w = [BIS.at(cv)]
            if NTP.n >= 1:
                w.append(NTE.at(NTP.n))
            b.on("tensor", (lambda e, j=jt: e.matmul(
                PS[0:128, 0:16], cmpb[0:16, j * 128:(j + 1) * 128], C["ident"][0:16, 0:16],
                start=True, stop=True)),
                wait=w, inc=NTP)
            b.on("vector", (lambda e, j=jt: e.tensor_scalar(
                hnT[:, j * 16:(j + 1) * 16], PS[0:128, 0:16], 1.0e9, -1.0e9, ALU.mult, ALU.add)),
                wait=[NTP.at(NTP.n)], inc=NTE)
        b.on("vector", lambda e: e.memset(bis[:, 7:8], 0.0), wait=[NTE.at(NTE.n)], inc=BIS)
        hist[("bis",)] = BIS.n

    def passB(h, vcol, kbuf, gate):
        for ts in range(4):
            pvq = PSq[2 + ts % 2]
            zq = PC[0:1, (ts % 2) * 512:(ts % 2) * 512 + 512]
            njt = 4 * ts + 4
            for jt in range(njt):
                d0 = 4 * ts - jt
                sq_ = PSq[jt % 2]
                wmm = [gate, NTE.at(hist[("ntr", h)]),
                       BEX.at(hist.get(("bex", h, ts, jt - 2),
                                       hist.get(("bexl", h, ts - 1), hist.get(("bexh", h - 1), 0)))),
                       BDV.at(hist.get(("bdv", h, ts, jt - 2),
                                       hist.get(("bdvl", h, ts - 1), hist.get(("bdvh", h - 1), 0))))]
                if is0:
                    wmm.append(BIS.at(hist.get(("bis",), 0)))
                b.on("tensor", (lambda e, j=jt, t_=ts, p=sq_, kb=kbuf: e.matmul(
                    p, kb[:, j * 128:(j + 1) * 128], qr[:, t_ * 512:(t_ + 1) * 512],
                    start=True, stop=False)), wait=wmm)
                b.on("tensor", (lambda e, t_=ts, p=sq_: e.matmul(
                    p, C["ones_1x128b"][0:1, :],
                    negmrow[0:1, t_ * 512:(t_ + 1) * 512],
                    start=False, stop=True)), inc=BMM)
                mv = BMM.n
                zpv_w = ZPV.at(hist.get(("zpv", h, ts, jt - 2),
                                        hist.get(("zpvl", h, ts - 1), hist.get(("zpvh", h - 1), 0))))
                if is0:
                    hncol = hnT[:, jt * 16 + h: jt * 16 + h + 1]
                    if d0 >= 3:
                        b.on("scalar", (lambda e, p=sq_, hc=hncol, buf=e2t[jt % 2]:
                                        e.activation(buf[:], p, AF.Exp, bias=hc)),
                             wait=[BMM.at(mv), zpv_w], inc=BEX)
                    else:
                        b.on("vector", (lambda e, p=sq_, hc=hncol, dd=d0, buf=s2a[jt % 2]:
                                        e.scalar_tensor_tensor(buf[:], p, hc, C[f"dg{dd}"][:],
                                                               ALU.add, ALU.add)),
                             wait=[BMM.at(mv), C["_cld"]], inc=BDV)
                        b.on("vector", (lambda e, p=sq_, dd=d0, buf=s2b[jt % 2]:
                                        e.tensor_tensor(buf[:], p, C[f"pat{dd}"][:], ALU.add)),
                             inc=BDV)
                        b.on("vector", (lambda e, a=s2a[jt % 2], bb=s2b[jt % 2]:
                                        e.tensor_tensor(a[:], a[:], bb[:], ALU.max)),
                             wait=[BDV.at(BDV.n)], inc=BDV)
                        b.on("scalar", (lambda e, a=s2a[jt % 2], buf=e2t[jt % 2]:
                                        e.activation(buf[:], a[:], AF.Exp)),
                             wait=[BDV.at(BDV.n), zpv_w], inc=BEX)
                else:
                    if d0 >= 1:
                        b.on("scalar", (lambda e, p=sq_, buf=e2t[jt % 2]:
                                        e.activation(buf[:], p, AF.Exp)),
                             wait=[BMM.at(mv), zpv_w], inc=BEX)
                    else:
                        b.on("vector", (lambda e, p=sq_, dd=d0, buf=s2a[jt % 2]:
                                        e.tensor_tensor(buf[:], p, C[f"cp{dd}"][:], ALU.add)),
                             wait=[BMM.at(mv), C["_cld"]], inc=BDV)
                        b.on("scalar", (lambda e, a=s2a[jt % 2], buf=e2t[jt % 2]:
                                        e.activation(buf[:], a[:], AF.Exp)),
                             wait=[BDV.at(BDV.n), zpv_w], inc=BEX)
                hist[("bex", h, ts, jt)] = BEX.n
                hist[("bdv", h, ts, jt)] = BDV.n
                b.on("tensor", (lambda e, j=jt, z=zq, buf=e2t[jt % 2], nj=njt: e.matmul(
                    z, C["ones_128x1b"][:], buf[:], start=(j == 0), stop=(j == nj - 1))),
                    wait=[BEX.at(BEX.n)] + ([EVC.at(hist.get(("evcz", h, ts - 2),
                                                              hist.get(("evczh", h - 1), 0)))] if jt == 0 else []))
                b.on("tensor", (lambda e, j=jt, p=pvq, vc_=vcol, buf=e2t[jt % 2], nj=njt: e.matmul(
                    p, v4[:, j * 512 + vc_ * 128: j * 512 + (vc_ + 1) * 128], buf[:],
                    start=(j == 0), stop=(j == nj - 1))),
                    wait=([VEV.at(hist[("vev_cur",)])] +
                          ([EVC.at(hist.get(("evcp", h, ts - 2), hist.get(("evcph", h - 1), 0)))]
                           if jt == 0 else [])),
                    inc=ZPV)
                hist[("zpv", h, ts, jt)] = ZPV.n
            hist[("bexl", h, ts)] = BEX.n
            hist[("bdvl", h, ts)] = BDV.n
            hist[("zpvl", h, ts)] = ZPV.n
            zv = ZPV.n
            b.on("vector", (lambda e, z=zq: e.reciprocal(zrow[:], z)), wait=[ZPV.at(zv)], inc=EVC)
            b.on("vector", (lambda e: e.tensor_copy(r2row[:], zrow[:])), wait=[EVC.at(EVC.n)], inc=EVC)
            hist[("evcz", h, ts)] = EVC.n
            b.on("tensor", (lambda e, t_=ts: e.matmul(PC[:, 1024 + (t_ % 2) * 512: 1536 + (t_ % 2) * 512],
                                               C["ones_1x128b"][0:1, :], r2row[:],
                                               start=True, stop=True)),
                 wait=[EVC.at(EVC.n), EVC.at(hist.get(("evcb", h, ts - 2),
                                                      hist.get(("evcbh", h - 1), 0)))], inc=EVC)
            b.on("vector", (lambda e, t_=ts: e.tensor_copy(
                r2b[:], PC[:, 1024 + (t_ % 2) * 512: 1536 + (t_ % 2) * 512])),
                wait=[EVC.at(EVC.n)], inc=EVC)
            hist[("evcb", h, ts)] = EVC.n
            b.on("vector", (lambda e, p=pvq: e.tensor_tensor(atn[:], p, r2b[:], ALU.mult)),
                 wait=[EVC.at(EVC.n), APW.at(hist.get(("apw", h, ts - 1),
                                                      hist.get(("apwh", h - 1), 0)))], inc=EVC)
            hist[("evcp", h, ts)] = EVC.n
            ev = EVC.n
            for q_ in range(4):
                b.dma("gpsimd", at_sp[ts * 4 + q_][:, h * 128:(h + 1) * 128],
                      atn[:, q_ * 128:(q_ + 1) * 128], wait=[EVC.at(ev)], inc=APW)
            hist[("apw", h, ts)] = APW.n
        hist[("evczh", h)] = EVC.n
        hist[("evcbh", h)] = EVC.n
        hist[("bexh", h)] = BEX.n
        hist[("bdvh", h)] = BDV.n
        hist[("zpvh", h)] = ZPV.n
        hist[("evcph", h)] = EVC.n
        hist[("apwh", h)] = APW.n

    if is0:
        for h in range(16):
            guard = {}
            if h > 0:
                guard["mm"] = [AEX.at(hist[("exp", h - 1, 15)])]
                guard["rope0"] = [SPL.at(hist.get(("spl", h - 1), 0))]
            hist[("rope_q", h)] = proj_head(qr, qph[h], cq_t, sq_t, None, 2 * h, guard)
            hist[("rope_k", h)] = proj_head(kr[0], kph[h], ck_t, sk_t, None, 2 * h + 1, {})
            b.dma("gpsimd", qk_sp[h], qr[:], wait=[RPE.at(hist[("rope_q", h)])], inc=SPL)
            b.dma("gpsimd", qk_sp[16 + h], kr[0][:], wait=[RPE.at(hist[("rope_k", h)])], inc=SPL)
            hist[("spl", h)] = SPL.n
            passA_L0(h)
        bisect()
        for h in range(16):
            g = h // 4
            if h % 4 == 0:
                vguard = {}
                if h > 0:
                    vguard["vmm"] = [ZPV.at(hist[("zpvh", h - 1)]), EVC.at(hist[("evcph", h - 1)])]
                    vguard["vload"] = [ZPV.at(hist[("zpvh", h - 1)])]
                compute_vgroup(g, g, vguard)
            wb = [SPL.at(SPL.n)]
            if h > 0:
                wb += [BMM.at(hist.get(("bmmh", h - 1), 0))]
            else:
                wb += [CSM.at(hist[("cs", 15, 15)])]
            b.dma("sync", qr[:], qk_sp[h], wait=wb, inc=BLD)
            b.dma("sync", kr[0][:], qk_sp[16 + h], inc=BLD)
            hist[("bld", h)] = BLD.n
            gB = [BIS.at(hist[("bis",)])]
            if h > 0:
                gB += [BMM.at(hist[("bmmh", h - 1)]), ZPV.at(hist[("zpvh", h - 1)])]
            negm_to_row(h, guard=gB)
            passB(h, h % 4, kr[0], BLD.at(hist[("bld", h)]))
            hist[("bmmh", h)] = BMM.n
    else:
        KV_N = int(os.environ.get("K_NKV", "4"))
        for g in range(KV_N):
            b.dma("gpsimd", bcol[:, 1:2], bkap[g * 128:(g + 1) * 128].rearrange("(p o) -> p o", o=1),
                  wait=[RPE.at(RPE.n)] + ln_guard, inc=BCL)
            bkv = BCL.n
            nobias = os.environ.get("K_NOBIAS") == "1"
            hist[("rope_kk", g)] = proj_head(kr[g], kph[g], ck_t, sk_t,
                                             None if nobias else bcol[:, 1:2], g,
                                             {"rope0": [BCL.at(bkv)]})
        if dbg_aps is not None and dbg_aps.get("kr"):
            for g in range(4):
                b.dma("gpsimd", dbg_aps["kr"][g], kr[g][:],
                      wait=[RPE.at(hist[("rope_kk", g)])], inc=OST)
        if os.environ.get("K_NOV") != "1":
            compute_vgroup(0, 0, {})
        if dbg_aps is not None:
            b.dma("gpsimd", dbg_aps["v4"], v4[:], wait=[VEV.at(VEV.n)], inc=OST)
        NH_DBG = int(os.environ.get("K_NHEADS", "16"))
        for h in range(NH_DBG):
            g = h // 4
            b.dma("gpsimd", bcol[:, 0:1], bqap[h * 128:(h + 1) * 128].rearrange("(p o) -> p o", o=1),
                  wait=[RPE.at(RPE.n)] + ln_guard, inc=BCL)
            bqv = BCL.n
            guard = {}
            if h > 0:
                guard["mm"] = [ADV.at(hist[("negmv", h - 1, 15)]), NTE.at(hist[("ntr", h - 1)]),
                               ZPV.at(hist[("zpvh", h - 1)]), EVC.at(hist[("evcph", h - 1)])]
            guard["rope0"] = [BCL.at(bqv)]
            hist[("rope_q", h)] = proj_head(qr, qph[h], cq_t, sq_t, bcol[:, 0:1], 4 + h, guard)
            hist[("pArb", h)] = ADV.n
            passA_L1(h, g)
            gB = [BMM.at(hist.get(("bmmh", h - 1), 0)), ZPV.at(hist.get(("zpvh", h - 1), 0))]
            negm_to_row(h, guard=gB)
            passB(h, g, kr[g], RPE.at(hist[("rope_q", h)]))
            hist[("bmmh", h)] = BMM.n

    if dbg_aps is not None:
        b.dma("gpsimd", dbg_aps["negm"], negm_allh[:], wait=[ADV.at(ADV.n)], inc=OST)
        b.dma("gpsimd", dbg_aps["qr"], qr[:], wait=[RPE.at(RPE.n)], inc=OST)
        b.dma("gpsimd", dbg_aps["nrow"], negmrow[:], wait=[NTE.at(NTE.n)], inc=OST)
        if is0:
            b.dma("gpsimd", dbg_aps["cs"], colsums[:], wait=[BIS.at(BIS.n)], inc=OST)
            b.dma("gpsimd", dbg_aps["hnt"], hnT[:], wait=[BIS.at(BIS.n)], inc=OST)
            b.dma("gpsimd", dbg_aps["bis"], bis[:], wait=[BIS.at(BIS.n)], inc=OST)
    sc2.close()
    scX.close()
    sc3 = ExitStack()
    outer.enter_context(sc3)
    opan = [sbo(f"opan{i}", [128, D], BF16, sc3) for i in range(16)]
    apan = [sbo(f"apan{i}", [128, D], BF16, sc3) for i in range(2)]
    xt3 = [sbo(f"xt3{i}", [128, D], F32, sc3) for i in range(2)]
    res3 = [sbo(f"res3{i}", [128, D], F32, sc3) for i in range(2)]
    bob = sbo("bob", [128, D], F32, sc3) if boap is not None else None
    barrier = [APW.at(APW.n), RPE.at(RPE.n), VEV.at(VEV.n), EVC.at(EVC.n),
               ZPV.at(ZPV.n), NTE.at(NTE.n), ADV.at(ADV.n),
               SMM.at(SMM.n), BEX.at(BEX.n), BDV.at(BDV.n), BMM.at(BMM.n),
               AEX.at(AEX.n), CSM.at(CSM.n), SPL.at(SPL.n), BLD.at(BLD.n),
               BVR.at(BVR.n), BCL.at(BCL.n), BIAC.at(BIAC.n),
               PRJ.at(PRJ.n), VMM.at(VMM.n), BIS.at(BIS.n),
               TREv.at(TREv.n), TRDv.at(TRDv.n), LNVv.at(LNVv.n), LNSv.at(LNSv.n),
               WQp[0].at(WQp[0].n), WQp[1].at(WQp[1].n), VG.at(VG.n),
               LDAp[0].at(LDAp[0].n), LDAp[1].at(LDAp[1].n)]
    if boap is not None:
        borow = sbo("borow", [1, D], F32, sc3)
        b.dma("gpsimd", borow[:], boap.rearrange("(o d) -> o d", o=1), wait=barrier, inc=BOR)
        bo0 = BOR.n
        for q_ in range(4):
            b.on("tensor", (lambda e, qq=q_: e.matmul(
                PC[:, 1024 + (qq % 2) * 512: 1536 + (qq % 2) * 512],
                C["ones_1x128f"][0:1, :], borow[0:1, qq * 512:(qq + 1) * 512],
                start=True, stop=True)),
                wait=[BOR.at(bo0), EVC.at(EVC.n), BIAC.at(BIAC.n)], inc=BIAC)
            b.on("vector", (lambda e, qq=q_: e.tensor_copy(
                bob[:, qq * 512:(qq + 1) * 512], PC[:, 1024 + (qq % 2) * 512: 1536 + (qq % 2) * 512])),
                wait=[BIAC.at(BIAC.n)], inc=BIAC)
    bo_done = BIAC.n
    for r in range(16):
        b.dma("sync", opan[r][:], oph[r], wait=barrier, inc=OPN)
    opl_all = OPN.n
    if os.environ.get("K_NOOP") == "1":
        for tt in range(16):
            X3c = X3p[tt % 2]
            b.dma("gpsimd", xt3[tt % 2][:], stream_in[tt * 128:(tt + 1) * 128, :],
                  wait=list(barrier) + ([OSp[tt % 2].at(hist.get(("ost", tt - 2), 0))] if tt >= 2 else []),
                  inc=X3c)
            b.dma("gpsimd", stream_out[tt * 128:(tt + 1) * 128, :], xt3[tt % 2][:],
                  wait=[X3c.at(X3c.n)], inc=OSp[tt % 2])
            hist[("ost", tt)] = OSp[tt % 2].n
        outer.close()
        return [(OSp[0], OSp[0].n), (OSp[1], OSp[1].n)]
    orsd = {}
    evc_all = EVC.n
    for tt in range(16):
        wld = list(barrier)
        if tt >= 2:
            wld.append(OMM.at(hist[("omm", tt - 2)]))
        APc = APp[tt % 2]
        b.dma("sync", apan[tt % 2][:], at_sp[tt], wait=wld, inc=APc)
        av = APc.n
        X3c = X3p[tt % 2]
        b.dma("gpsimd", xt3[tt % 2][:], stream_in[tt * 128:(tt + 1) * 128, :],
              wait=([OSp[tt % 2].at(hist.get(("ost", tt - 2), 0))] if tt >= 2 else []), inc=X3c)
        lv = X3c.n
        for os_ in range(4):
            gidx = tt * 4 + os_
            pq = PSq[gidx % 4]
            wmm = [APc.at(av), OPN.at(opl_all), EVC.at(evc_all)]
            if gidx >= 4:
                wmm.append(ORS.at(orsd[gidx - 4]))
            for r in range(16):
                b.on("tensor", (lambda e, rr=r, p=pq, o=os_, t_=tt: e.matmul(
                    p, apan[t_ % 2][:, rr * 128:(rr + 1) * 128],
                    opan[rr][:, o * 512:(o + 1) * 512],
                    start=(rr == 0), stop=(rr == 15))),
                    wait=(wmm if r == 0 else ()), inc=(OMM if r == 15 else None))
            omv = OMM.n
            if boap is not None:
                b.on("vector", (lambda e, p=pq, o=os_, t_=tt: e.tensor_tensor(
                    res3[t_ % 2][:, o * 512:(o + 1) * 512], p, bob[:, o * 512:(o + 1) * 512],
                    ALU.add)), wait=[OMM.at(omv), X3c.at(lv), BIAC.at(bo_done)], inc=ORS)
                b.on("vector", (lambda e, p=pq, o=os_, t_=tt: e.tensor_tensor(
                    res3[t_ % 2][:, o * 512:(o + 1) * 512], res3[t_ % 2][:, o * 512:(o + 1) * 512],
                    xt3[t_ % 2][:, o * 512:(o + 1) * 512], ALU.add)),
                    wait=[ORS.at(ORS.n)], inc=ORS)
            else:
                b.on("vector", (lambda e, p=pq, o=os_, t_=tt: e.tensor_tensor(
                    res3[t_ % 2][:, o * 512:(o + 1) * 512], p,
                    xt3[t_ % 2][:, o * 512:(o + 1) * 512], ALU.add)),
                    wait=[OMM.at(omv), X3c.at(lv)], inc=ORS)
            orsd[gidx] = ORS.n
        hist[("omm", tt)] = OMM.n
        rv = ORS.n
        b.dma("gpsimd", stream_out[tt * 128:(tt + 1) * 128, :], res3[tt % 2][:],
              wait=[ORS.at(rv)], inc=OSp[tt % 2])
        hist[("ost", tt)] = OSp[tt % 2].n
    outer.close()
    return [(OSp[0], OSp[0].n), (OSp[1], OSp[1].n)]


def build_attn(layer, dbg=False):
    nc = bass.Bass("TRN2", target_bir_lowering=False)
    stack = ExitStack()
    with stack:
        b = B(nc, stack)
        names = ["ident", "identf", "ones_1x128b", "ones_1x128f", "ones_128x1b",
                 "cp0", "cp-1", "cp-2", "cp-3", "cpA"]
        if layer == 0:
            names += ["cq0", "sq0", "ck0", "sk0"]
            names += [f"pat{d}" for d in range(-3, 3)] + [f"dg{d}" for d in range(-3, 3)]
        else:
            names += ["cq1", "sq1", "ck1", "sk1"]
        C, cld = load_consts(b, names)
        C["_cld"] = cld
        fc, fcs = make_fc(b, [EPS, -0.5, 0.5])
        fc["_sem"] = fcs
        sin = b.dram("sin", [S, D], F32, "ExternalInput")
        sout = b.dram("sout", [S, D], F32, "ExternalOutput")
        if layer == 0:
            qph = b.dram("qph", [16, 128, D], BF16, "ExternalInput")
            kph = b.dram("kph", [16, 128, D], BF16, "ExternalInput")
            vph = b.dram("vph", [4, 128, MLP1], BF16, "ExternalInput")
            oph = b.dram("oph", [16, 128, D], BF16, "ExternalInput")
            qk_sp = b.dram("qk_sp", [32, 128, D], BF16, "ExternalOutput" if dbg else "Internal")
            at_sp = b.dram("at_sp", [16, 128, D], BF16, "ExternalOutput" if dbg else "Internal")
            dbg_aps = None
            if dbg:
                dbg_aps = {"negm": b.dram("negm_d", [128, 256], F32, "ExternalOutput"),
                           "qr": b.dram("qr_d", [128, S], BF16, "ExternalOutput"),
                           "nrow": b.dram("nrow_d", [1, S], BF16, "ExternalOutput"),
                           "cs": b.dram("cs_d", [16, S], BF16, "ExternalOutput"),
                           "hnt": b.dram("hnt_d", [128, 256], BF16, "ExternalOutput"),
                           "bis": b.dram("bis_d", [16, 8], F32, "ExternalOutput"),
                           "kr": []}
            fw = emit_attn(b, C, fc, "a_", sin, sout, 0, qph, kph, vph, oph, qk_sp, at_sp,
                           dbg_aps=dbg_aps)
        else:
            qph = b.dram("qph", [16, 128, D], BF16, "ExternalInput")
            kph = b.dram("kph", [4, 128, D], BF16, "ExternalInput")
            vph = b.dram("vph", [1, 128, MLP1], BF16, "ExternalInput")
            oph = b.dram("oph", [16, 128, D], BF16, "ExternalInput")
            at_sp = b.dram("at_sp", [16, 128, D], BF16, "ExternalOutput" if dbg else "Internal")
            bq = b.dram("bq", [D], F32, "ExternalInput")
            bk = b.dram("bk", [512], F32, "ExternalInput")
            bv = b.dram("bv", [512], F32, "ExternalInput")
            bo = b.dram("bo", [D], F32, "ExternalInput")
            dbg_aps = None
            if dbg:
                dbg_aps = {"negm": b.dram("negm_d", [128, 256], F32, "ExternalOutput"),
                           "kr": [b.dram(f"kr_d{g}", [128, S], BF16, "ExternalOutput") for g in range(4)],
                           "qr": b.dram("qr_d", [128, S], BF16, "ExternalOutput"),
                           "v4": b.dram("v4_d", [128, 8192], BF16, "ExternalOutput"),
                           "nrow": b.dram("nrow_d", [1, S], BF16, "ExternalOutput"),
                           "dmp": b.dram("dmp_d", [128, 2048], F32, "ExternalOutput")}
            fw = emit_attn(b, C, fc, "a_", sin, sout, 1, qph, kph, vph, oph, None, at_sp,
                           bq, bk, bv, bo, dbg_aps)
        b.finalize(final_waits=fw)
    return nc


MEGA_CONSTS = ["ident", "identf", "ones_1x128b", "ones_1x128f", "ones_128x1b",
               "cp0", "cp-1", "cp-2", "cp-3", "cpA",
               "cq0", "sq0", "ck0", "sk0", "cq1", "sq1", "ck1", "sk1"] + \
              [f"pat{d}" for d in range(-3, 3)] + [f"dg{d}" for d in range(-3, 3)]


def build_forward():
    nc = bass.Bass("TRN2", target_bir_lowering=False)
    stack = ExitStack()
    with stack:
        b = B(nc, stack)
        b.share_sems = True
        C, cld = load_consts(b, ["ident", "identf", "ones_1x128b", "ones_1x128f",
                                 "ones_128x1b"])
        C["_cld"] = cld
        fc, fcs = make_fc(b, [EPS, -0.5, 0.5])
        fc["_sem"] = fcs
        sin = b.dram("sin", [S, D], F32, "ExternalInput")
        sout = b.dram("sout", [S, D], F32, "ExternalOutput")
        st_ = [b.dram(f"stream{i}", [S, D], F32, "Internal") for i in range(5)]
        gph = b.dram("gph", [MLP0 // 128, 128, D], BF16, "ExternalInput")
        uph = b.dram("uph", [MLP0 // 128, 128, D], BF16, "ExternalInput")
        dph = b.dram("dph", [MLP0 // 128, 128, D], BF16, "ExternalInput")
        w1ph = b.dram("w1ph", [MLP1 // 128, 128, D], BF16, "ExternalInput")
        w2ph = b.dram("w2ph", [MLP1 // 128, 128, D], BF16, "ExternalInput")
        b1 = b.dram("b1", [MLP1], F32, "ExternalInput")
        b2 = b.dram("b2", [D], F32, "ExternalInput")
        q0ph = b.dram("q0ph", [16, 128, D], BF16, "ExternalInput")
        k0ph = b.dram("k0ph", [16, 128, D], BF16, "ExternalInput")
        v0ph = b.dram("v0ph", [4, 128, MLP1], BF16, "ExternalInput")
        o0ph = b.dram("o0ph", [16, 128, D], BF16, "ExternalInput")
        q1ph = b.dram("q1ph", [16, 128, D], BF16, "ExternalInput")
        k1ph = b.dram("k1ph", [4, 128, D], BF16, "ExternalInput")
        v1ph = b.dram("v1ph", [1, 128, MLP1], BF16, "ExternalInput")
        o1ph = b.dram("o1ph", [16, 128, D], BF16, "ExternalInput")
        bq = b.dram("bq", [D], F32, "ExternalInput")
        bk = b.dram("bk", [512], F32, "ExternalInput")
        bv = b.dram("bv", [512], F32, "ExternalInput")
        bo = b.dram("bo", [D], F32, "ExternalInput")
        qk_sp = b.dram("qk_sp", [32, 128, D], BF16, "Internal")
        at_sp = b.dram("at_sp", [16, 128, D], BF16, "Internal")

        fw = emit_ffn(b, C, fc, "p0_", sin, st_[0], MLP0, gph, uph, dph)
        b.set_entry_guard(fw)
        csc1 = ExitStack()
        Ca0, _ = load_consts(b, ["cq0", "sq0", "ck0", "sk0", "cpA", "cp0", "cp-1",
                                 "cp-2", "cp-3"] +
                             [f"pat{d}" for d in range(-3, 3)] +
                             [f"dg{d}" for d in range(-3, 3)], csc1)
        C1 = dict(C)
        C1.update(Ca0)
        fw = emit_attn(b, C1, fc, "p1_", st_[0], st_[1], 0, q0ph, k0ph, v0ph, o0ph,
                       qk_sp, at_sp)
        csc1.close()
        b.set_entry_guard(fw)
        fw = emit_ffn(b, C, fc, "p2_", st_[1], st_[2], MLP0, gph, uph, dph)
        b.set_entry_guard(fw)
        fw = emit_ffn(b, C, fc, "p3_", st_[2], st_[3], MLP1, w1ph, None, w2ph, b1, b2)
        b.set_entry_guard(fw)
        csc2 = ExitStack()
        Ca1, _ = load_consts(b, ["cq1", "sq1", "ck1", "sk1", "cpA", "cp0", "cp-1",
                                 "cp-2", "cp-3"], csc2)
        C2 = dict(C)
        C2.update(Ca1)
        fw = emit_attn(b, C2, fc, "p4_", st_[3], st_[4], 1, q1ph, k1ph, v1ph, o1ph,
                       None, at_sp, bq, bk, bv, bo)
        csc2.close()
        b.set_entry_guard(fw)
        fw = emit_ffn(b, C, fc, "p5_", st_[4], sout, MLP1, w1ph, None, w2ph, b1, b2)
        b.finalize(final_waits=fw)
    return nc


def _panelize_T(W):
    O, K = W.shape
    x = W.reshape(O // 128, 128, K // 128, 128)
    return np.ascontiguousarray(x.transpose(0, 3, 2, 1).reshape(O // 128, 128, K))


def _panelize_N(W):
    K, O = W.shape
    x = W.reshape(K // 128, 128, O // 128, 128)
    return np.ascontiguousarray(x.transpose(2, 1, 0, 3).reshape(O // 128, 128, K))


def prep_inputs_for_core(inp, c):
    hp = {}
    hp["gWp"] = np.ascontiguousarray(inp["L0_ffn_gate_W_packed"][c * 2048:(c + 1) * 2048])
    hp["gSs"] = np.ascontiguousarray(inp["L0_ffn_gate_scales"][c * 2048:(c + 1) * 2048])
    hp["uWp"] = np.ascontiguousarray(inp["L0_ffn_up_W_packed"][c * 2048:(c + 1) * 2048])
    hp["uSs"] = np.ascontiguousarray(inp["L0_ffn_up_scales"][c * 2048:(c + 1) * 2048])
    hp["dWp"] = np.ascontiguousarray(inp["L0_ffn_down_W_packed"][:, c * 1024:(c + 1) * 1024])
    hp["dSs"] = np.ascontiguousarray(inp["L0_ffn_down_scales"][:, c * 64:(c + 1) * 64])
    w1pan = _panelize_T(inp["L1_ffn_w1"])
    w2pan = _panelize_N(inp["L1_ffn_w2"].T.copy()) if False else None
    w2 = inp["L1_ffn_w2"]
    w2pan = np.ascontiguousarray(w2.T.reshape(MLP1 // 128, 128, D))
    hp["w1p"] = np.ascontiguousarray(w1pan[c * 8:(c + 1) * 8])
    hp["w2p"] = np.ascontiguousarray(w2pan[c * 8:(c + 1) * 8])
    q0 = _panelize_T(inp["L0_attn_Wq"])
    k0 = _panelize_T(inp["L0_attn_Wk"])
    Wv = inp["L0_attn_Wv"]
    vx = Wv.reshape(4, 4, 128, 16, 128)
    v0 = np.ascontiguousarray(vx.transpose(0, 4, 3, 1, 2).reshape(4, 128, MLP1))
    o0 = _panelize_N(inp["L0_attn_Wo"].T.copy()) if False else None
    o0 = np.ascontiguousarray(inp["L0_attn_Wo"].T.reshape(D // 128, 128, D))
    q1 = _panelize_N(inp["L1_attn_Wq"])
    k1 = _panelize_N(inp["L1_attn_Wk"])
    Wv1 = inp["L1_attn_Wv"]
    v1x = Wv1.reshape(16, 128, 4, 128)
    v1 = np.ascontiguousarray(v1x.transpose(1, 0, 2, 3).reshape(1, 128, MLP1))
    o1 = inp["L1_attn_Wo"].reshape(D // 128, 128, D)
    hp["q0p"] = np.ascontiguousarray(q0[c * 2:(c + 1) * 2])
    hp["k0p"] = np.ascontiguousarray(k0[c * 2:(c + 1) * 2])
    hp["v0p"] = np.ascontiguousarray(v0[c:c + 1]) if c < 4 else np.zeros((1, 128, MLP1), np.float32)
    hp["o0p"] = np.ascontiguousarray(o0[c * 2:(c + 1) * 2])
    hp["q1p"] = np.ascontiguousarray(q1[c * 2:(c + 1) * 2])
    hp["k1p"] = np.ascontiguousarray(k1[c:c + 1]) if c < 4 else np.zeros((1, 128, D), np.float32)
    hp["v1p"] = np.ascontiguousarray(v1) if c == 4 else np.zeros((1, 128, MLP1), np.float32)
    hp["o1p"] = np.ascontiguousarray(o1[c * 2:(c + 1) * 2])
    return hp


def assemble_stashes(prep_results):
    st = {}
    for k in ("gT", "uT", "dT"):
        st[k] = np.concatenate([r[k] for r in prep_results], 0)
    st["w1T"] = np.concatenate([r["w1T"] for r in prep_results], 0)
    st["w2T"] = np.concatenate([r["w2T"] for r in prep_results], 0)
    for k in ("q0T", "k0T", "o0T", "q1T", "o1T"):
        st[k] = np.concatenate([r[k] for r in prep_results], 0)
    st["v0T"] = np.concatenate([r["v0T"] for r in prep_results[:4]], 0)
    st["k1T"] = np.concatenate([r["k1T"] for r in prep_results[:4]], 0)
    st["v1T"] = np.asarray(prep_results[4]["v1T"])
    return st


_PROG_CACHE = {}


def _prog(name, fn):
    if name not in _PROG_CACHE:
        _PROG_CACHE[name] = fn()
    return _PROG_CACHE[name]


ATTN0_CONSTS = ["ident", "identf", "ones_1x128b", "ones_1x128f", "ones_128x1b",
                "cp0", "cp-1", "cp-2", "cp-3", "cpA", "cq0", "sq0", "ck0", "sk0"] +                [f"pat{d}" for d in range(-3, 3)] + [f"dg{d}" for d in range(-3, 3)]
ATTN1_CONSTS = ["ident", "identf", "ones_1x128b", "ones_1x128f", "ones_128x1b",
                "cp0", "cp-1", "cp-2", "cp-3", "cpA", "cq1", "sq1", "ck1", "sk1"]
FFN_CONSTS = ["ident", "ones_1x128b", "ones_1x128f"]
PREP_CONSTS = ["identf", "ident", "ones_1x128f", "R64", "R64h"]


def kernel(**inputs):
    inp = {k: np.asarray(v) for k, v in inputs.items()}
    x0 = inp["L0_attn_hidden_states"]

    nc1 = _prog("prep1", build_prep1)
    maps1 = [{"w1s": np.ascontiguousarray(inp["L1_ffn_w1"][c * 1024:(c + 1) * 1024]),
              "w2s": np.ascontiguousarray(inp["L1_ffn_w2"][c * 256:(c + 1) * 256])}
             for c in range(8)]
    r1 = run_bass_kernel_spmd(nc1, maps1, core_ids=list(range(8)))
    sums = np.stack([np.asarray(r["sums"])[0] for r in r1.results])
    absums = np.ascontiguousarray(sums.reshape(1, 16)).astype(np.float32)

    nc2 = _prog("prep2", build_prep2)
    cst_p = const_inputs(PREP_CONSTS)
    maps2 = []
    for c in range(8):
        m = prep_inputs_for_core(inp, c)
        m.update(cst_p)
        m["absums"] = absums
        maps2.append(m)
    r2 = run_bass_kernel_spmd(nc2, maps2, core_ids=list(range(8)))
    st = assemble_stashes(r2.results)

    nf = _prog("forward", build_forward)
    cst = const_inputs(MEGA_CONSTS)
    maps = []
    for c in range(8):
        m = dict(cst)
        m.update(sin=np.ascontiguousarray(x0[c]),
                 gph=st["gT"], uph=st["uT"], dph=st["dT"],
                 w1ph=st["w1T"], w2ph=st["w2T"],
                 b1=inp["L1_ffn_b1"], b2=inp["L1_ffn_b2"],
                 q0ph=st["q0T"], k0ph=st["k0T"], v0ph=st["v0T"], o0ph=st["o0T"],
                 q1ph=st["q1T"], k1ph=st["k1T"],
                 v1ph=st["v1T"].reshape(1, 128, MLP1), o1ph=st["o1T"],
                 bq=inp["L1_attn_bq"], bk=inp["L1_attn_bk"],
                 bv=inp["L1_attn_bv"], bo=inp["L1_attn_bo"])
        maps.append(m)
    r = run_bass_kernel_spmd(nf, maps, core_ids=list(range(8)))
    return np.stack([np.asarray(x["sout"]) for x in r.results], 0)


if __name__ == "__main__":
    nc = build_prep1()
    print("prep1 builds OK")
    nc2 = build_prep2()
    print("prep2 builds OK")
